# version 09
# baseline (speedup 1.0000x reference)
import numpy as np
import ml_dtypes
import concourse.bass as bass
import concourse.mybir as mybir
from concourse.bass_utils import run_bass_kernel_spmd

F32 = mybir.dt.float32
BF16 = mybir.dt.bfloat16
AF = mybir.ActivationFunctionType
ALU = mybir.AluOpType
AX = mybir.AxisListType.X
BF = ml_dtypes.bfloat16

NB = 2
S = 2048
D = 1024
E = 160
INTER = 512
NT = 16
NW = 8
KCH = 8
ICH = 4
WPB = 161 * NW
UNROLL_EXPERTS = True
DEBUG = False

SB0 = 16512
OFF_XNAT = SB0 + 0
OFF_ROUT = SB0 + 32768
OFF_XIN = SB0 + 98304
OFF_SCR = SB0 + 106496
OFF_XNT = SB0 + 110592
OFF_SLOTW = SB0 + 143360
OFF_WQT = SB0 + 176128
OFF_SLOTC = SB0 + 192512
OFF_QTT = SB0 + 208896


def build_program():
    nc = bass.Bass("TRN2", target_bir_lowering=False)

    x_d = nc.dram_tensor("x", [NB, S, D], F32, kind="ExternalInput").ap()
    y_d = nc.dram_tensor("y", [NB, S, D], F32, kind="ExternalOutput").ap()
    wqT_d = nc.dram_tensor("wqT", [128, 8192], F32, kind="ExternalInput").ap()
    wkT_d = nc.dram_tensor("wkT", [128, 8192], F32, kind="ExternalInput").ap()
    bq_d = nc.dram_tensor("bqr", [128, 8], F32, kind="ExternalInput").ap()
    bk_d = nc.dram_tensor("bkr", [128, 8], F32, kind="ExternalInput").ap()
    rwT_d = nc.dram_tensor("rwT", [128, 1280], F32, kind="ExternalInput").ap()
    bias_d = nc.dram_tensor("biasbc", [128, E], F32, kind="ExternalInput").ap()
    wg_d = nc.dram_tensor("wgT", [E, 128, 4096], F32, kind="ExternalInput").ap()
    wd_d = nc.dram_tensor("wdT", [E, 128, 4096], F32, kind="ExternalInput").ap()
    wgs_d = nc.dram_tensor("wgsT", [128, 4096], F32, kind="ExternalInput").ap()
    wds_d = nc.dram_tensor("wdsT", [128, 4096], F32, kind="ExternalInput").ap()
    id_d = nc.dram_tensor("ident", [128, 128], BF16, kind="ExternalInput").ap()
    if DEBUG:
        d_xn = nc.dram_tensor("d_xn", [NB, 128, 16384], BF16, kind="ExternalOutput").ap()
        d_xnT = nc.dram_tensor("d_xnT", [NB, 128, 16384], BF16, kind="ExternalOutput").ap()
        d_comb = nc.dram_tensor("d_comb", [NB, 128, 2560], F32, kind="ExternalOutput").ap()
        d_g = nc.dram_tensor("d_g", [NB, 128, 16384], F32, kind="ExternalOutput").ap()
        d_kT = nc.dram_tensor("d_kT", [NB, 128, 16384], BF16, kind="ExternalOutput").ap()
        d_qt0 = nc.dram_tensor("d_qt0", [NB, 128, 1024], BF16, kind="ExternalOutput").ap()
        d_P0 = nc.dram_tensor("d_P0", [NB, 128, 2048], F32, kind="ExternalOutput").ap()
        s_dbg = nc.alloc_semaphore("s_dbg")

    A_ = nc.alloc_sbuf_tensor_at
    xnat = A_("xnat", [128, 16384], BF16, offset=OFF_XNAT)
    routed = A_("routed", [128, 16384], F32, offset=OFF_ROUT)
    xin = [A_(f"xin{i}", [128, 1024], F32, offset=OFF_XIN + i * 4096) for i in range(2)]
    qTt = A_("qTt", [128, 1024], BF16, offset=OFF_SCR)
    bias_sb = A_("bias_sb", [128, 160], F32, offset=OFF_SCR + 2048)
    bq_sb = A_("bq_sb", [128, 8], F32, offset=OFF_SCR + 2688)
    bk_sb = A_("bk_sb", [128, 8], F32, offset=OFF_SCR + 2720)
    ident = A_("ident_sb", [128, 128], BF16, offset=OFF_SCR + 2752)
    sc_base = OFF_SCR + 3008
    names = ["s1", "s2", "meanv", "varv", "msq", "rstdv", "sqv", "rmaxv",
             "negmax", "rsumv", "zinv", "meanrow", "tsumv", "tinvv", "inv32"]
    scr = {}
    for i, n in enumerate(names):
        scr[n] = A_(n, [128, 1], F32, offset=sc_base + 32 * i)
    top8 = A_("top8", [128, 8], F32, offset=sc_base + 32 * len(names))
    xnT = A_("xnT", [128, 16384], BF16, offset=OFF_XNT)
    wg_sb = [A_(f"wg{i}", [128, 4096], BF16, offset=OFF_SLOTW + i * 16384) for i in range(2)]
    wd_sb = [A_(f"wd{i}", [128, 4096], BF16, offset=OFF_SLOTW + 8192 + i * 16384) for i in range(2)]
    kT = A_("kT", [128, 16384], BF16, offset=OFF_SLOTW)
    wqT_sb = A_("wqT_sb", [128, 8192], BF16, offset=OFF_WQT)
    comb = A_("comb", [128, 2560], F32, offset=OFF_SLOTC)
    h1w_offs = [OFF_SLOTC + 10240, OFF_SLOTC + 12288, OFF_SLOTC + 14336, OFF_QTT]
    h1w = [A_(f"h1w{i}", [128, 1024], BF16, offset=h1w_offs[i]) for i in range(4)]
    rwT_sb = A_("rwT_sb", [128, 1280], BF16, offset=OFF_SLOTC + 10240)
    ttmp = A_("ttmp", [128, 1024], F32, offset=OFF_SLOTC)
    probs_pre = A_("probs_pre", [128, 160], F32, offset=SB0 + 210944)
    probs = A_("probs", [128, 160], F32, offset=SB0 + 211584)
    ge = A_("ge", [128, 160], F32, offset=SB0 + 210944)
    wkT_sb = A_("wkT_sb", [128, 8192], BF16, offset=OFF_SLOTC)
    P = A_("P", [128, 2048], F32, offset=OFF_SLOTC)
    Pb = A_("Pb", [128, 2048], BF16, offset=OFF_SLOTC + 8192)
    PTs = A_("PTs", [128, 2048], BF16, offset=OFF_SLOTC + 12288)

    h1_t = nc.alloc_psum_tensor("h1t", [128, 2048], F32)
    dn0_t = nc.alloc_psum_tensor("dn0t", [128, 1024], F32)
    dn1_t = nc.alloc_psum_tensor("dn1t", [128, 1024], F32)
    nc.psum_base = 0
    arena = nc.alloc_psum_tensor("arena", [128, 3072], F32)
    ptb = nc.alloc_psum_tensor("ptb", [128, 2048], BF16)
    nc.psum_base = 0
    _ph0 = nc.alloc_psum_tensor("ph0", [128, 3072], F32)
    ptf = nc.alloc_psum_tensor("ptf", [128, 1024], F32)
    h1_ps = h1_t[:]
    dn_ps = [dn0_t[:], dn1_t[:]]
    lg_ps = arena[:, 0:160]
    qt_ps = arena[:, 0:1024]
    sc_ps = arena[:, 0:2048]
    s_ps = arena[:, 2048:3072]
    kt_ps = [arena[:, 2048:2560], arena[:, 2560:3072]]
    gb_ps = [h1_t[:, 0:1024], h1_t[:, 1024:2048]]

    sem_names = ["s_wdma", "s_xL", "s_xA", "s_out", "s_cst",
                 "s_pe_ln", "s_pe_lg", "s_pe_h1", "s_pe_dn",
                 "s_pe_qt", "s_pe_kt", "s_pe_sc", "s_pe_pt", "s_pe_av",
                 "s_act_rstd", "s_act_sig", "s_act_silu", "s_act_g",
                 "s_act_exp", "s_act_qk", "s_act_qt",
                 "s_dve_ln1", "s_dve_xn", "s_dve_xnT", "s_dve_lg",
                 "s_dve_sm1", "s_dve_pb", "s_dve_pt", "s_dve_epi",
                 "s_dve_evac", "s_dve_g"]
    sem = {n: nc.alloc_semaphore(n) for n in sem_names}

    def wdma_rwt_ready(b):
        return 16 * (b * 325 + 1)

    def wdma_shared_ready(b):
        return 16 * (b * 325 + 3)

    def wdma_exp_ready(b, e):
        return 16 * (b * 325 + 2 * e + 5)

    def wdma_qk_ready(b):
        return 16 * (325 * (b + 1))

    def dn_cnt(b, w, m):
        return (b * WPB + w) * 2 + m + 1

    def evac_cnt(b, w, m):
        return (b * WPB + w) * 2 + m + 1

    def silu_cnt(b, w):
        return b * WPB + w + 1


    with nc.Block() as blk:

        @blk.sync
        def _(sy):
            sy.dma_start(out=ident[:], in_=id_d).then_inc(sem["s_cst"], 16)
            sy.dma_start(out=bias_sb[:], in_=bias_d).then_inc(sem["s_cst"], 16)
            sy.dma_start(out=bq_sb[:], in_=bq_d).then_inc(sem["s_cst"], 16)
            sy.dma_start(out=bk_sb[:], in_=bk_d).then_inc(sem["s_cst"], 16)
            for b in range(NB):
                if b > 0:
                    sy.wait_ge(sem["s_out"], 16 * b * NT)
                for t in range(NT):
                    war = b * NT + t - 1
                    if war > 0:
                        sy.wait_ge(sem["s_dve_xn"], war)
                    sy.dma_start(out=xin[t % 2][:], in_=x_d[b, t * 128:(t + 1) * 128, :]
                                 ).then_inc(sem["s_xL"], 16)
                if DEBUG:
                    sy.wait_ge(sem["s_dve_xnT"], (b + 1) * NT)
                    sy.dma_start(out=d_xn[b], in_=xnat[:]).then_inc(s_dbg, 16)
                    sy.dma_start(out=d_xnT[b], in_=xnT[:]).then_inc(s_dbg, 16)
                    sy.wait_ge(sem["s_dve_evac"], b * 2 * WPB + 17)
                    sy.dma_start(out=d_comb[b], in_=comb[:]).then_inc(s_dbg, 16)
                    sy.wait_ge(sem["s_dve_g"], (b + 1) * NT)
                    sy.dma_start(out=d_g[b], in_=routed[:]).then_inc(s_dbg, 16)
                    sy.wait_ge(sem["s_act_qk"], (b + 1) * 32)
                    sy.dma_start(out=d_kT[b], in_=kT[:]).then_inc(s_dbg, 16)
                    sy.wait_ge(sem["s_act_qt"], b * NT + 1)
                    sy.dma_start(out=d_qt0[b], in_=qTt[:]).then_inc(s_dbg, 16)
                    sy.wait_ge(sem["s_dve_pb"], b * NT + 1)
                    sy.dma_start(out=d_P0[b], in_=P[:]).then_inc(s_dbg, 16)
                sy.wait_ge(sem["s_dve_xn"], (b + 1) * NT)
                for qt in range(NT):
                    war = b * NT + qt - 1
                    if war > 0:
                        sy.wait_ge(sem["s_dve_epi"], war)
                        sy.wait_ge(sem["s_out"], 16 * war)
                    sy.dma_start(out=xin[qt % 2][:], in_=x_d[b, qt * 128:(qt + 1) * 128, :]
                                 ).then_inc(sem["s_xA"], 16)
                    sy.wait_ge(sem["s_dve_epi"], b * NT + qt + 1)
                    sy.dma_start(out=y_d[b, qt * 128:(qt + 1) * 128, :], in_=xin[qt % 2][:]
                                 ).then_inc(sem["s_out"], 16)
            sy.wait_ge(sem["s_out"], 16 * NB * NT)

        @blk.gpsimd
        def _(g):
            for b in range(NB):
                if b > 0:
                    g.wait_ge(sem["s_dve_pt"], 2 * b * NT)
                g.dma_start(out=rwT_sb[:], in_=rwT_d).then_inc(sem["s_wdma"], 16)
                if b > 0:
                    g.wait_ge(sem["s_pe_sc"], b * NT)
                g.dma_start(out=wg_sb[1][:], in_=wgs_d).then_inc(sem["s_wdma"], 16)
                g.dma_start(out=wd_sb[1][:], in_=wds_d).then_inc(sem["s_wdma"], 16)
                for e in range(E):
                    if e >= 2:
                        g.wait_ge(sem["s_pe_dn"], 2 * (b * WPB + e * NW))
                    elif e == 1:
                        g.wait_ge(sem["s_pe_dn"], 2 * (b * WPB + NW))
                    elif b > 0:
                        g.wait_ge(sem["s_pe_sc"], b * NT)
                    g.dma_start(out=wg_sb[e % 2][:], in_=wg_d[e]).then_inc(sem["s_wdma"], 16)
                    g.dma_start(out=wd_sb[e % 2][:], in_=wd_d[e]).then_inc(sem["s_wdma"], 16)
                g.wait_ge(sem["s_pe_dn"], 2 * (b + 1) * WPB)
                g.wait_ge(sem["s_dve_evac"], 2 * (b + 1) * WPB)
                g.dma_start(out=wqT_sb[:], in_=wqT_d).then_inc(sem["s_wdma"], 16)
                g.dma_start(out=wkT_sb[:], in_=wkT_d).then_inc(sem["s_wdma"], 16)

        @blk.tensor
        def _(t):
            t.wait_ge(sem["s_cst"], 64)
            for b in range(NB):
                if b > 0:
                    t.wait_ge(sem["s_act_exp"], b * NT)
                    t.wait_ge(sem["s_dve_epi"], b * NT)
                    t.wait_ge(sem["s_dve_pt"], 2 * b * NT)
                    t.wait_ge(sem["s_dve_g"], b * NT)
                for ti in range(NT):
                    t.wait_ge(sem["s_dve_xn"], b * NT + ti + 1)
                    if b * NT + ti > 0:
                        t.wait_ge(sem["s_dve_xnT"], b * NT + ti)
                    for c in range(KCH):
                        mm = t.matmul(ptf[:, c * 128:(c + 1) * 128],
                                      xnat[:, ti * 1024 + c * 128: ti * 1024 + (c + 1) * 128],
                                      ident[:], start=True, stop=True)
                    mm.then_inc(sem["s_pe_ln"], 1)
                t.wait_ge(sem["s_wdma"], wdma_rwt_ready(b))
                for ti in range(NT):
                    t.wait_ge(sem["s_dve_xnT"], b * NT + ti + 1)
                    if b * NT + ti > 0:
                        t.wait_ge(sem["s_dve_lg"], b * NT + ti)
                    for c in range(KCH):
                        mm = t.matmul(lg_ps, xnT[:, c * 2048 + ti * 128: c * 2048 + (ti + 1) * 128],
                                      rwT_sb[:, c * 160:(c + 1) * 160],
                                      start=(c == 0), stop=(c == KCH - 1))
                    mm.then_inc(sem["s_pe_lg"], 1)
                t.wait_ge(sem["s_dve_lg"], (b + 1) * NT)
                t.wait_ge(sem["s_wdma"], wdma_shared_ready(b))

                def gate(eng, wslot, jp):
                    for ich in range(ICH):
                        for c in range(KCH):
                            mm = eng.matmul(
                                h1_ps[:, ich * 512:(ich + 1) * 512],
                                wg_sb[wslot][:, c * 512 + ich * 128: c * 512 + (ich + 1) * 128],
                                xnT[:, c * 2048 + jp * 512: c * 2048 + (jp + 1) * 512],
                                start=(c == 0), stop=(c == KCH - 1))
                    mm.then_inc(sem["s_pe_h1"], 1)

                def down(eng, wslot, par, evac_base):
                    for m in range(2):
                        if evac_base + m > 0:
                            eng.wait_ge(sem["s_dve_evac"], evac_base + m)
                        for ich in range(ICH):
                            for half in range(2):
                                mm = eng.matmul(
                                    dn_ps[m][:, half * 512:(half + 1) * 512],
                                    h1w[par][:, ich * 256 + m * 128: ich * 256 + m * 128 + 128],
                                    wd_sb[wslot][:, ich * 1024 + half * 512: ich * 1024 + (half + 1) * 512],
                                    start=(ich == 0), stop=(ich == ICH - 1))
                        mm.then_inc(sem["s_pe_dn"], 1)

                def down_window(eng, wl):
                    wl_e = wl // NW - 1
                    dslot = 1 if wl_e < 0 else wl_e % 2
                    eng.wait_ge(sem["s_act_silu"], b * WPB + wl + 1)
                    down(eng, dslot, wl % 4, 2 * (b * WPB + wl) - 1)

                def expert_gwins(eng, es):
                    e = es - 1
                    gslot = 1 if e < 0 else e % 2
                    if e >= 0:
                        eng.wait_ge(sem["s_wdma"], wdma_exp_ready(b, e))
                    for jp in range(4):
                        W = es * NW + 2 * jp
                        if W - 2 >= 0:
                            down_window(eng, W - 2)
                        if W > 0:
                            eng.wait_ge(sem["s_act_silu"], b * WPB + W)
                        gate(eng, gslot, jp)
                        if W - 1 >= 0:
                            down_window(eng, W - 1)
                for es in range(161):
                    expert_gwins(t, es)
                down_window(t, WPB - 2)
                down_window(t, WPB - 1)

                t.wait_ge(sem["s_act_silu"], (b + 1) * WPB)
                t.wait_ge(sem["s_dve_evac"], 2 * (b + 1) * WPB)
                t.wait_ge(sem["s_dve_g"], (b + 1) * NT)
                t.wait_ge(sem["s_wdma"], wdma_qk_ready(b))
                for idx in range(32):
                    ec, tc = idx // 4, idx % 4
                    if b * 32 + idx > 1:
                        t.wait_ge(sem["s_act_qk"], b * 32 + idx - 1)
                    for c in range(KCH):
                        mm = t.matmul(kt_ps[idx % 2],
                                      wkT_sb[:, c * 1024 + ec * 128: c * 1024 + (ec + 1) * 128],
                                      xnT[:, c * 2048 + tc * 512: c * 2048 + (tc + 1) * 512],
                                      start=(c == 0), stop=(c == KCH - 1))
                    mm.then_inc(sem["s_pe_kt"], 1)
                def emit_qTt_scores(qt):
                    if DEBUG and qt == 1:
                        t.wait_ge(s_dbg, 112 * b + 96)
                    t.wait_ge(sem["s_act_exp"], b * NT + qt)
                    t.wait_ge(sem["s_act_qt"], b * NT + qt)
                    for ec in range(KCH):
                        for c in range(KCH):
                            mm = t.matmul(qt_ps[:, ec * 128:(ec + 1) * 128],
                                          wqT_sb[:, c * 1024 + ec * 128: c * 1024 + (ec + 1) * 128],
                                          xnT[:, c * 2048 + qt * 128: c * 2048 + (qt + 1) * 128],
                                          start=(c == 0), stop=(c == KCH - 1))
                    mm.then_inc(sem["s_pe_qt"], 1)
                    t.wait_ge(sem["s_act_qt"], b * NT + qt + 1)
                    if qt == 0:
                        t.wait_ge(sem["s_act_qk"], (b + 1) * 32)
                    for kc in range(4):
                        for ec in range(KCH):
                            mm = t.matmul(sc_ps[:, kc * 512:(kc + 1) * 512],
                                          qTt[:, ec * 128:(ec + 1) * 128],
                                          kT[:, ec * 2048 + kc * 512: ec * 2048 + (kc + 1) * 512],
                                          start=(ec == 0), stop=(ec == KCH - 1))
                    mm.then_inc(sem["s_pe_sc"], 1)

                def emit_pt_av(qt):
                    t.wait_ge(sem["s_dve_pb"], b * NT + qt + 1)
                    for h in range(2):
                        t.wait_ge(sem["s_dve_pt"], 2 * (b * NT + qt) + h)
                        for kt_i in range(8 * h, 8 * h + 8):
                            mm = t.matmul(ptf[:, (kt_i % 8) * 128:(kt_i % 8 + 1) * 128],
                                          Pb[:, kt_i * 128:(kt_i + 1) * 128], ident[:],
                                          start=True, stop=True)
                        mm.then_inc(sem["s_pe_pt"], 1)
                    t.wait_ge(sem["s_dve_pt"], 2 * (b * NT + qt + 1))
                    t.wait_ge(sem["s_dve_epi"], b * NT + qt)
                    for dh in range(2):
                        for kt_i in range(NT):
                            mm = t.matmul(s_ps[:, dh * 512:(dh + 1) * 512],
                                          PTs[:, kt_i * 128:(kt_i + 1) * 128],
                                          xnat[:, kt_i * 1024 + dh * 512: kt_i * 1024 + dh * 512 + 512],
                                          start=(kt_i == 0), stop=(kt_i == NT - 1))
                    mm.then_inc(sem["s_pe_av"], 1)

                for qt in range(NT):
                    emit_qTt_scores(qt)
                    if qt >= 1:
                        emit_pt_av(qt - 1)
                emit_pt_av(NT - 1)

        @blk.scalar
        def _(s):
            s.wait_ge(sem["s_cst"], 64)
            for b in range(NB):
                for ti in range(NT):
                    s.wait_ge(sem["s_dve_ln1"], b * NT + ti + 1)
                    s.activation(scr["sqv"][:], scr["varv"][:], AF.Sqrt
                                 ).then_inc(sem["s_act_rstd"], 1)
                for ti in range(NT):
                    s.wait_ge(sem["s_dve_lg"], b * NT + ti + 1)
                    s.activation(probs[:], probs_pre[:], AF.Sigmoid
                                 ).then_inc(sem["s_act_sig"], 1)

                h1_3d = h1_t.ap().rearrange("p (i z) -> p i z", i=ICH)
                h1w_3d = [h1w[par].ap().rearrange("p (i z) -> p i z", i=ICH) for par in range(4)]
                GPB = 161 * 4
                for w in range(WPB):
                    hpar = w % 2
                    s.wait_ge(sem["s_pe_h1"], b * GPB + w // 2 + 1)
                    war = 2 * (b * WPB + w) - 6
                    if war > 0:
                        s.wait_ge(sem["s_pe_dn"], war)
                    s.activation(h1w_3d[w % 4], h1_3d[:, :, hpar * 256:(hpar + 1) * 256],
                                 AF.Silu).then_inc(sem["s_act_silu"], 1)

                s.wait_ge(sem["s_dve_evac"], 2 * (b + 1) * WPB)
                for ti in range(NT):
                    if b * NT + ti > 1:
                        s.wait_ge(sem["s_dve_g"], b * NT + ti - 1)
                    s.activation(gb_ps[ti % 2], routed[:, ti * 1024:(ti + 1) * 1024], AF.Silu
                                 ).then_inc(sem["s_act_g"], 1)

                for idx in range(32):
                    ec, tc = idx // 4, idx % 4
                    s.wait_ge(sem["s_pe_kt"], b * 32 + idx + 1)
                    s.add(kT[:, ec * 2048 + tc * 512: ec * 2048 + (tc + 1) * 512],
                          kt_ps[idx % 2], bk_sb[:, ec:ec + 1]).then_inc(sem["s_act_qk"], 1)
                for qt in range(NT):
                    s.wait_ge(sem["s_pe_qt"], b * NT + qt + 1)
                    for ec in range(KCH):
                        ins = s.add(qTt[:, ec * 128:(ec + 1) * 128],
                                    qt_ps[:, ec * 128:(ec + 1) * 128], bq_sb[:, ec:ec + 1])
                    ins.then_inc(sem["s_act_qt"], 1)
                    if DEBUG and qt == 1:
                        s.wait_ge(s_dbg, 112 * b + 112)
                    s.wait_ge(sem["s_pe_sc"], b * NT + qt + 1)
                    s.wait_ge(sem["s_dve_sm1"], b * NT + qt + 1)
                    s.activation(P[:], sc_ps, AF.Exp, bias=scr["negmax"][:], scale=scr["inv32"][:]
                                 ).then_inc(sem["s_act_exp"], 1)

        @blk.vector
        def _(v):
            v.memset(scr["inv32"][:], 1.0 / 32.0)
            v.wait_ge(sem["s_cst"], 64)
            for b in range(NB):
                for ti in range(NT):
                    xb = xin[ti % 2]
                    v.wait_ge(sem["s_xL"], 16 * (b * NT + ti + 1))
                    v.reduce_sum(scr["s1"][:], xb[:], axis=AX)
                    v.tensor_mul(ttmp[:], xb[:], xb[:])
                    v.drain()
                    v.reduce_sum(scr["s2"][:], ttmp[:], axis=AX)
                    v.drain()
                    v.tensor_scalar_mul(scr["meanv"][:], scr["s1"][:], 1.0 / D)
                    v.drain()
                    v.tensor_mul(scr["msq"][:], scr["meanv"][:], scr["meanv"][:])
                    v.tensor_scalar_mul(scr["varv"][:], scr["s2"][:], 1.0 / D)
                    v.drain()
                    v.tensor_sub(scr["varv"][:], scr["varv"][:], scr["msq"][:])
                    v.drain()
                    v.tensor_scalar_add(scr["varv"][:], scr["varv"][:], 1e-5
                                        ).then_inc(sem["s_dve_ln1"], 1)
                    v.wait_ge(sem["s_act_rstd"], b * NT + ti + 1)
                    v.reciprocal(scr["rstdv"][:], scr["sqv"][:])
                    v.drain()
                    v.tensor_scalar(xnat[:, ti * 1024:(ti + 1) * 1024], xb[:],
                                    scr["meanv"][:], scr["rstdv"][:],
                                    op0=ALU.subtract, op1=ALU.mult).then_inc(sem["s_dve_xn"], 1)
                    v.wait_ge(sem["s_pe_ln"], b * NT + ti + 1)
                    dst = xnT.ap().rearrange("p (c z) -> p c z", c=KCH)[:, :, ti * 128:(ti + 1) * 128]
                    src = ptf.ap().rearrange("p (c z) -> p c z", c=KCH)
                    v.tensor_copy(dst, src).then_inc(sem["s_dve_xnT"], 1)
                for ti in range(NT):
                    v.wait_ge(sem["s_pe_lg"], b * NT + ti + 1)
                    v.tensor_add(probs_pre[:], lg_ps, bias_sb[:]).then_inc(sem["s_dve_lg"], 1)
                    v.wait_ge(sem["s_act_sig"], b * NT + ti + 1)
                    v.max(out=top8[:], in_=probs[:])
                    v.drain()
                    v.reduce_sum(scr["tsumv"][:], top8[:], axis=AX)
                    v.drain()
                    v.tensor_scalar_add(scr["tsumv"][:], scr["tsumv"][:], 1e-9)
                    v.drain()
                    v.reciprocal(scr["tinvv"][:], scr["tsumv"][:])
                    v.drain()
                    v.tensor_scalar_mul(scr["tinvv"][:], scr["tinvv"][:], 2.5)
                    v.drain()
                    v.tensor_scalar(ge[:], probs[:], top8[:, 7:8], None, op0=ALU.is_ge)
                    v.drain()
                    ct = comb[:, ti * 160:(ti + 1) * 160]
                    v.tensor_mul(ct, ge[:], probs[:])
                    v.drain()
                    v.tensor_scalar(ct, ct, scr["tinvv"][:], None, op0=ALU.mult)
                    v.drain()

                for j in range(NW):
                    for m in range(2):
                        w = j
                        v.wait_ge(sem["s_pe_dn"], dn_cnt(b, w, m))
                        v.tensor_copy(routed[:, (2 * j + m) * 1024:(2 * j + m + 1) * 1024],
                                      dn_ps[m]).then_inc(sem["s_dve_evac"], 1)

                def evac_expert(eng, i_or_e, sub):
                    for j in range(NW):
                        for m in range(2):
                            if UNROLL_EXPERTS:
                                e = i_or_e
                            else:
                                e = 2 * i_or_e + sub
                            w = (e + 1) * NW + j
                            eng.wait_ge(sem["s_pe_dn"], dn_cnt(b, w, m))
                            rt = routed[:, (2 * j + m) * 1024:(2 * j + m + 1) * 1024]
                            if UNROLL_EXPERTS:
                                cap = comb[:, (2 * j + m) * 160 + e: (2 * j + m) * 160 + e + 1]
                            else:
                                cap = comb[:, bass.ts((2 * j + m) * 160 + e, 1)]
                            eng.scalar_tensor_tensor(out=rt, in0=dn_ps[m], scalar=cap, in1=rt,
                                                     op0=ALU.mult, op1=ALU.add
                                                     ).then_inc(sem["s_dve_evac"], 1)
                if UNROLL_EXPERTS:
                    for e in range(E):
                        evac_expert(v, e, 0)
                else:
                    with v.Fori(0, E // 2) as i:
                        evac_expert(v, i, 0)
                        evac_expert(v, i, 1)

                for ti in range(NT):
                    v.wait_ge(sem["s_act_g"], b * NT + ti + 1)
                    v.tensor_copy(routed[:, ti * 1024:(ti + 1) * 1024], gb_ps[ti % 2]
                                  ).then_inc(sem["s_dve_g"], 1)

                for qt in range(NT):
                    v.wait_ge(sem["s_pe_sc"], b * NT + qt + 1)
                    v.reduce_max(scr["rmaxv"][:], sc_ps, axis=AX)
                    v.drain()
                    v.tensor_scalar_mul(scr["negmax"][:], scr["rmaxv"][:], -1.0 / 32.0)
                    v.drain().then_inc(sem["s_dve_sm1"], 1)
                    v.wait_ge(sem["s_act_exp"], b * NT + qt + 1)
                    v.reduce_sum(scr["rsumv"][:], P[:], axis=AX)
                    v.drain()
                    v.reciprocal(scr["zinv"][:], scr["rsumv"][:])
                    v.drain()
                    v.tensor_scalar(P[:], P[:], scr["zinv"][:], None, op0=ALU.mult)
                    v.drain()
                    v.reduce_sum(scr["meanrow"][:], P[:], axis=AX)
                    v.drain()
                    v.tensor_scalar_add(scr["meanrow"][:], scr["meanrow"][:], 1.0)
                    v.drain()
                    v.tensor_scalar_mul(scr["meanrow"][:], scr["meanrow"][:], 1.0 / S)
                    v.drain()
                    v.tensor_scalar(Pb[:], P[:], scr["meanrow"][:], None, op0=ALU.subtract)
                    v.drain()
                    v.tensor_add(Pb[:, qt * 128:(qt + 1) * 128],
                                 Pb[:, qt * 128:(qt + 1) * 128], ident[:])
                    v.drain().then_inc(sem["s_dve_pb"], 1)
                    for h in range(2):
                        v.wait_ge(sem["s_pe_pt"], 2 * (b * NT + qt) + h + 1)
                        v.tensor_copy(PTs[:, h * 1024:(h + 1) * 1024], ptf[:]
                                      ).then_inc(sem["s_dve_pt"], 1)
                    v.wait_ge(sem["s_pe_av"], b * NT + qt + 1)
                    v.wait_ge(sem["s_xA"], 16 * (b * NT + qt + 1))
                    rt = routed[:, qt * 1024:(qt + 1) * 1024]
                    v.tensor_mul(rt, s_ps, rt)
                    v.drain()
                    v.tensor_add(xin[qt % 2][:], rt, xin[qt % 2][:]).then_inc(sem["s_dve_epi"], 1)

    return nc


def kernel(L0_attn_x, L0_attn_Wq, L0_attn_bq, L0_attn_Wk, L0_attn_bk,
           L0_ffn_router_w, L0_ffn_expert_bias, L0_ffn_w_gate, L0_ffn_w_down,
           L0_ffn_w_gate_shared, L0_ffn_w_down_shared):
    def pmajor(a, nch):
        C = a.shape[0] // 128
        return np.ascontiguousarray(
            a.reshape(C, 128, -1).transpose(1, 0, 2).reshape(128, -1))

    x = np.ascontiguousarray(np.asarray(L0_attn_x, dtype=np.float32))
    wqT = pmajor(np.asarray(L0_attn_Wq, dtype=np.float32).T, KCH)
    wkT = pmajor(np.asarray(L0_attn_Wk, dtype=np.float32).T, KCH)
    bqr = np.ascontiguousarray(np.asarray(L0_attn_bq, dtype=np.float32).reshape(8, 128).T)
    bkr = np.ascontiguousarray(np.asarray(L0_attn_bk, dtype=np.float32).reshape(8, 128).T)
    rwT = pmajor(np.asarray(L0_ffn_router_w, dtype=np.float32).T, KCH)
    biasbc = np.ascontiguousarray(
        np.broadcast_to(np.asarray(L0_ffn_expert_bias, dtype=np.float32), (128, E)))
    wgT3 = np.asarray(L0_ffn_w_gate, dtype=np.float32).transpose(0, 2, 1)
    wgT = np.ascontiguousarray(
        wgT3.reshape(E, KCH, 128, INTER).transpose(0, 2, 1, 3).reshape(E, 128, 4096))
    wdT3 = np.asarray(L0_ffn_w_down, dtype=np.float32).transpose(0, 2, 1)
    wdT = np.ascontiguousarray(
        wdT3.reshape(E, ICH, 128, D).transpose(0, 2, 1, 3).reshape(E, 128, 4096))
    wgsT = pmajor(np.asarray(L0_ffn_w_gate_shared, dtype=np.float32).T, KCH)
    wdsT = pmajor(np.asarray(L0_ffn_w_down_shared, dtype=np.float32).T, ICH)
    identh = np.eye(128, dtype=np.float32).astype(BF)

    nc = build_program()
    in_maps = []
    for c in range(8):
        in_maps.append(dict(x=np.ascontiguousarray(x[2 * c:2 * c + 2]),
                            wqT=wqT, wkT=wkT, bqr=bqr, bkr=bkr, rwT=rwT,
                            biasbc=biasbc, wgT=wgT, wdT=wdT, wgsT=wgsT, wdsT=wdsT,
                            ident=identh))
    res = run_bass_kernel_spmd(nc, in_maps, core_ids=list(range(8)))
    return np.concatenate([np.asarray(r["y"]) for r in res.results], axis=0)


# version 11
# speedup vs baseline: 1.0202x; 1.0202x over previous
import numpy as np
import ml_dtypes
import concourse.bass as bass
import concourse.mybir as mybir
from concourse.bass_utils import run_bass_kernel_spmd

F32 = mybir.dt.float32
BF16 = mybir.dt.bfloat16
AF = mybir.ActivationFunctionType
ALU = mybir.AluOpType
AX = mybir.AxisListType.X
BF = ml_dtypes.bfloat16

NB = 2
S = 2048
D = 1024
E = 160
INTER = 512
NT = 16
NW = 8
KCH = 8
ICH = 4
WPB = 161 * NW
UNROLL_EXPERTS = True
DEBUG = False

SB0 = 16512
OFF_XNAT = SB0 + 0
OFF_ROUT = SB0 + 32768
OFF_XIN = SB0 + 98304
OFF_SCR = SB0 + 106496
OFF_XNT = SB0 + 110592
OFF_SLOTW = SB0 + 143360
OFF_WQT = SB0 + 176128
OFF_SLOTC = SB0 + 192512
OFF_QTT = SB0 + 208896


def build_program():
    nc = bass.Bass("TRN2", target_bir_lowering=False)

    x_d = nc.dram_tensor("x", [NB, S, D], F32, kind="ExternalInput").ap()
    y_d = nc.dram_tensor("y", [NB, S, D], F32, kind="ExternalOutput").ap()
    wqT_d = nc.dram_tensor("wqT", [128, 8192], F32, kind="ExternalInput").ap()
    wkT_d = nc.dram_tensor("wkT", [128, 8192], F32, kind="ExternalInput").ap()
    bq_d = nc.dram_tensor("bqr", [128, 8], F32, kind="ExternalInput").ap()
    bk_d = nc.dram_tensor("bkr", [128, 8], F32, kind="ExternalInput").ap()
    rwT_d = nc.dram_tensor("rwT", [128, 1280], F32, kind="ExternalInput").ap()
    bias_d = nc.dram_tensor("biasbc", [128, E], F32, kind="ExternalInput").ap()
    wg_d = nc.dram_tensor("wgT", [E, 128, 4096], F32, kind="ExternalInput").ap()
    wd_d = nc.dram_tensor("wdT", [E, 128, 4096], F32, kind="ExternalInput").ap()
    wgs_d = nc.dram_tensor("wgsT", [128, 4096], F32, kind="ExternalInput").ap()
    wds_d = nc.dram_tensor("wdsT", [128, 4096], F32, kind="ExternalInput").ap()
    id_d = nc.dram_tensor("ident", [128, 128], BF16, kind="ExternalInput").ap()
    if DEBUG:
        d_xn = nc.dram_tensor("d_xn", [NB, 128, 16384], BF16, kind="ExternalOutput").ap()
        d_xnT = nc.dram_tensor("d_xnT", [NB, 128, 16384], BF16, kind="ExternalOutput").ap()
        d_comb = nc.dram_tensor("d_comb", [NB, 128, 2560], F32, kind="ExternalOutput").ap()
        d_g = nc.dram_tensor("d_g", [NB, 128, 16384], F32, kind="ExternalOutput").ap()
        d_kT = nc.dram_tensor("d_kT", [NB, 128, 16384], BF16, kind="ExternalOutput").ap()
        d_qt0 = nc.dram_tensor("d_qt0", [NB, 128, 1024], BF16, kind="ExternalOutput").ap()
        d_P0 = nc.dram_tensor("d_P0", [NB, 128, 2048], F32, kind="ExternalOutput").ap()
        s_dbg = nc.alloc_semaphore("s_dbg")

    A_ = nc.alloc_sbuf_tensor_at
    xnat = A_("xnat", [128, 16384], BF16, offset=OFF_XNAT)
    routed = A_("routed", [128, 16384], F32, offset=OFF_ROUT)
    xin = [A_(f"xin{i}", [128, 1024], F32, offset=OFF_XIN + i * 4096) for i in range(2)]
    qTt = A_("qTt", [128, 1024], BF16, offset=OFF_SCR)
    bias_sb = A_("bias_sb", [128, 160], F32, offset=OFF_SCR + 2048)
    bq_sb = A_("bq_sb", [128, 8], F32, offset=OFF_SCR + 2688)
    bk_sb = A_("bk_sb", [128, 8], F32, offset=OFF_SCR + 2720)
    ident = A_("ident_sb", [128, 128], BF16, offset=OFF_SCR + 2752)
    sc_base = OFF_SCR + 3008
    names = ["s1", "s2", "meanv", "varv", "msq", "rstdv", "sqv", "rmaxv",
             "negmax", "rsumv", "zinv", "meanrow", "tsumv", "tinvv", "inv32"]
    scr = {}
    for i, n in enumerate(names):
        scr[n] = A_(n, [128, 1], F32, offset=sc_base + 32 * i)
    top8 = A_("top8", [128, 8], F32, offset=sc_base + 32 * len(names))
    xnT = A_("xnT", [128, 16384], BF16, offset=OFF_XNT)
    wg_sb = [A_(f"wg{i}", [128, 4096], BF16, offset=OFF_SLOTW + i * 16384) for i in range(2)]
    wd_sb = [A_(f"wd{i}", [128, 4096], BF16, offset=OFF_SLOTW + 8192 + i * 16384) for i in range(2)]
    kT = A_("kT", [128, 16384], BF16, offset=OFF_SLOTW)
    wqT_sb = A_("wqT_sb", [128, 8192], BF16, offset=OFF_WQT)
    comb = A_("comb", [128, 2560], F32, offset=OFF_SLOTC)
    h1w_offs = [OFF_SLOTC + 10240, OFF_SLOTC + 12288, OFF_SLOTC + 14336, OFF_QTT]
    h1w = [A_(f"h1w{i}", [128, 1024], BF16, offset=h1w_offs[i]) for i in range(4)]
    rwT_sb = A_("rwT_sb", [128, 1280], BF16, offset=OFF_SLOTC + 10240)
    ttmp = A_("ttmp", [128, 1024], F32, offset=OFF_SLOTC)
    probs_pre = A_("probs_pre", [128, 160], F32, offset=SB0 + 210944)
    probs = A_("probs", [128, 160], F32, offset=SB0 + 211584)
    ge = A_("ge", [128, 160], F32, offset=SB0 + 210944)
    wkT_sb = A_("wkT_sb", [128, 8192], BF16, offset=OFF_SLOTC)
    P = A_("P", [128, 2048], F32, offset=OFF_SLOTC)
    Pb = A_("Pb", [128, 2048], BF16, offset=OFF_SLOTC + 8192)
    PTs = A_("PTs", [128, 2048], BF16, offset=OFF_SLOTC + 12288)

    h1_t = nc.alloc_psum_tensor("h1t", [128, 2048], F32)
    dn0_t = nc.alloc_psum_tensor("dn0t", [128, 1024], F32)
    dn1_t = nc.alloc_psum_tensor("dn1t", [128, 1024], F32)
    nc.psum_base = 0
    arena = nc.alloc_psum_tensor("arena", [128, 3072], F32)
    ptb = nc.alloc_psum_tensor("ptb", [128, 2048], BF16)
    nc.psum_base = 0
    _ph0 = nc.alloc_psum_tensor("ph0", [128, 3072], F32)
    ptf = nc.alloc_psum_tensor("ptf", [128, 1024], F32)
    h1_ps = h1_t[:]
    dn_ps = [dn0_t[:], dn1_t[:]]
    lg_ps = arena[:, 0:160]
    qt_ps = arena[:, 0:1024]
    sc_ps = arena[:, 0:2048]
    s_ps = arena[:, 2048:3072]
    kt_ps = [arena[:, 2048:2560], arena[:, 2560:3072]]
    gb_ps = [h1_t[:, 0:1024], h1_t[:, 1024:2048]]

    sem_names = ["s_wdma", "s_xL", "s_xA", "s_out", "s_cst",
                 "s_pe_ln", "s_pe_lg", "s_pe_h1", "s_pe_dn",
                 "s_pe_qt", "s_pe_kt", "s_pe_sc", "s_pe_pt", "s_pe_av",
                 "s_act_rstd", "s_act_sig", "s_act_silu", "s_act_g",
                 "s_act_exp", "s_act_qk", "s_act_qt", "s_act_sq",
                 "s_dve_ln1", "s_dve_xn", "s_dve_xnT", "s_dve_lg",
                 "s_dve_sm1", "s_dve_pb", "s_dve_pt", "s_dve_epi",
                 "s_dve_evac", "s_dve_g"]
    sem = {n: nc.alloc_semaphore(n) for n in sem_names}

    def wdma_rwt_ready(b):
        return 16 * (b * 325 + 1)

    def wdma_shared_ready(b):
        return 16 * (b * 325 + 3)

    def wdma_exp_ready(b, e):
        return 16 * (b * 325 + 2 * e + 5)

    def wdma_qk_ready(b):
        return 16 * (325 * (b + 1))

    def dn_cnt(b, w, m):
        return (b * WPB + w) * 2 + m + 1

    def evac_cnt(b, w, m):
        return (b * WPB + w) * 2 + m + 1

    def silu_cnt(b, w):
        return b * WPB + w + 1


    with nc.Block() as blk:

        @blk.sync
        def _(sy):
            sy.dma_start(out=ident[:], in_=id_d).then_inc(sem["s_cst"], 16)
            sy.dma_start(out=bias_sb[:], in_=bias_d).then_inc(sem["s_cst"], 16)
            sy.dma_start(out=bq_sb[:], in_=bq_d).then_inc(sem["s_cst"], 16)
            sy.dma_start(out=bk_sb[:], in_=bk_d).then_inc(sem["s_cst"], 16)
            for b in range(NB):
                if b > 0:
                    sy.wait_ge(sem["s_out"], 16 * b * NT)
                for t in range(NT):
                    war = b * NT + t - 1
                    if war > 0:
                        sy.wait_ge(sem["s_dve_xn"], war)
                        sy.wait_ge(sem["s_act_sq"], war)
                    sy.dma_start(out=xin[t % 2][:], in_=x_d[b, t * 128:(t + 1) * 128, :]
                                 ).then_inc(sem["s_xL"], 16)
                if DEBUG:
                    sy.wait_ge(sem["s_dve_xnT"], (b + 1) * NT)
                    sy.dma_start(out=d_xn[b], in_=xnat[:]).then_inc(s_dbg, 16)
                    sy.dma_start(out=d_xnT[b], in_=xnT[:]).then_inc(s_dbg, 16)
                    sy.wait_ge(sem["s_dve_evac"], b * 2 * WPB + 17)
                    sy.dma_start(out=d_comb[b], in_=comb[:]).then_inc(s_dbg, 16)
                    sy.wait_ge(sem["s_dve_g"], (b + 1) * NT)
                    sy.dma_start(out=d_g[b], in_=routed[:]).then_inc(s_dbg, 16)
                    sy.wait_ge(sem["s_act_qk"], (b + 1) * 32)
                    sy.dma_start(out=d_kT[b], in_=kT[:]).then_inc(s_dbg, 16)
                    sy.wait_ge(sem["s_act_qt"], b * NT + 1)
                    sy.dma_start(out=d_qt0[b], in_=qTt[:]).then_inc(s_dbg, 16)
                    sy.wait_ge(sem["s_dve_pb"], b * NT + 1)
                    sy.dma_start(out=d_P0[b], in_=P[:]).then_inc(s_dbg, 16)
                sy.wait_ge(sem["s_dve_xn"], (b + 1) * NT)
                for qt in range(NT):
                    war = b * NT + qt - 1
                    if war > 0:
                        sy.wait_ge(sem["s_dve_epi"], war)
                        sy.wait_ge(sem["s_out"], 16 * war)
                    sy.dma_start(out=xin[qt % 2][:], in_=x_d[b, qt * 128:(qt + 1) * 128, :]
                                 ).then_inc(sem["s_xA"], 16)
                    sy.wait_ge(sem["s_dve_epi"], b * NT + qt + 1)
                    sy.dma_start(out=y_d[b, qt * 128:(qt + 1) * 128, :], in_=xin[qt % 2][:]
                                 ).then_inc(sem["s_out"], 16)
            sy.wait_ge(sem["s_out"], 16 * NB * NT)

        @blk.gpsimd
        def _(g):
            for b in range(NB):
                if b > 0:
                    g.wait_ge(sem["s_dve_pt"], 2 * b * NT)
                g.dma_start(out=rwT_sb[:], in_=rwT_d).then_inc(sem["s_wdma"], 16)
                if b > 0:
                    g.wait_ge(sem["s_pe_sc"], b * NT)
                g.dma_start(out=wg_sb[1][:], in_=wgs_d).then_inc(sem["s_wdma"], 16)
                g.dma_start(out=wd_sb[1][:], in_=wds_d).then_inc(sem["s_wdma"], 16)
                for e in range(E):
                    if e >= 2:
                        g.wait_ge(sem["s_pe_dn"], 2 * (b * WPB + e * NW))
                    elif e == 1:
                        g.wait_ge(sem["s_pe_dn"], 2 * (b * WPB + NW))
                    elif b > 0:
                        g.wait_ge(sem["s_pe_sc"], b * NT)
                    g.dma_start(out=wg_sb[e % 2][:], in_=wg_d[e]).then_inc(sem["s_wdma"], 16)
                    g.dma_start(out=wd_sb[e % 2][:], in_=wd_d[e]).then_inc(sem["s_wdma"], 16)
                g.wait_ge(sem["s_pe_dn"], 2 * (b + 1) * WPB)
                g.wait_ge(sem["s_dve_evac"], 2 * (b + 1) * WPB)
                g.dma_start(out=wqT_sb[:], in_=wqT_d).then_inc(sem["s_wdma"], 16)
                g.dma_start(out=wkT_sb[:], in_=wkT_d).then_inc(sem["s_wdma"], 16)

        @blk.tensor
        def _(t):
            t.wait_ge(sem["s_cst"], 64)
            for b in range(NB):
                if b > 0:
                    t.wait_ge(sem["s_act_exp"], b * NT)
                    t.wait_ge(sem["s_dve_epi"], b * NT)
                    t.wait_ge(sem["s_dve_pt"], 2 * b * NT)
                    t.wait_ge(sem["s_dve_g"], b * NT)
                for ti in range(NT):
                    t.wait_ge(sem["s_dve_xn"], b * NT + ti + 1)
                    if b * NT + ti > 0:
                        t.wait_ge(sem["s_dve_xnT"], b * NT + ti)
                    for c in range(KCH):
                        mm = t.matmul(ptf[:, c * 128:(c + 1) * 128],
                                      xnat[:, ti * 1024 + c * 128: ti * 1024 + (c + 1) * 128],
                                      ident[:], start=True, stop=True)
                    mm.then_inc(sem["s_pe_ln"], 1)
                t.wait_ge(sem["s_wdma"], wdma_rwt_ready(b))
                for ti in range(NT):
                    t.wait_ge(sem["s_dve_xnT"], b * NT + ti + 1)
                    if b * NT + ti > 0:
                        t.wait_ge(sem["s_dve_lg"], b * NT + ti)
                    for c in range(KCH):
                        mm = t.matmul(lg_ps, xnT[:, c * 2048 + ti * 128: c * 2048 + (ti + 1) * 128],
                                      rwT_sb[:, c * 160:(c + 1) * 160],
                                      start=(c == 0), stop=(c == KCH - 1))
                    mm.then_inc(sem["s_pe_lg"], 1)
                t.wait_ge(sem["s_dve_lg"], (b + 1) * NT)
                t.wait_ge(sem["s_wdma"], wdma_shared_ready(b))

                def gate(eng, wslot, jp):
                    for ich in range(ICH):
                        for c in range(KCH):
                            mm = eng.matmul(
                                h1_ps[:, ich * 512:(ich + 1) * 512],
                                wg_sb[wslot][:, c * 512 + ich * 128: c * 512 + (ich + 1) * 128],
                                xnT[:, c * 2048 + jp * 512: c * 2048 + (jp + 1) * 512],
                                start=(c == 0), stop=(c == KCH - 1))
                    mm.then_inc(sem["s_pe_h1"], 1)

                def down(eng, wslot, par, evac_base):
                    for m in range(2):
                        if evac_base + m > 0:
                            eng.wait_ge(sem["s_dve_evac"], evac_base + m)
                        for ich in range(ICH):
                            for half in range(2):
                                mm = eng.matmul(
                                    dn_ps[m][:, half * 512:(half + 1) * 512],
                                    h1w[par][:, ich * 256 + m * 128: ich * 256 + m * 128 + 128],
                                    wd_sb[wslot][:, ich * 1024 + half * 512: ich * 1024 + (half + 1) * 512],
                                    start=(ich == 0), stop=(ich == ICH - 1))
                        mm.then_inc(sem["s_pe_dn"], 1)

                def down_window(eng, wl):
                    wl_e = wl // NW - 1
                    dslot = 1 if wl_e < 0 else wl_e % 2
                    eng.wait_ge(sem["s_act_silu"], b * WPB + wl + 1)
                    down(eng, dslot, wl % 4, 2 * (b * WPB + wl) - 1)

                def expert_gwins(eng, es):
                    e = es - 1
                    gslot = 1 if e < 0 else e % 2
                    if e >= 0:
                        eng.wait_ge(sem["s_wdma"], wdma_exp_ready(b, e))
                    for jp in range(4):
                        W = es * NW + 2 * jp
                        if W - 2 >= 0:
                            down_window(eng, W - 2)
                        if W > 0:
                            eng.wait_ge(sem["s_act_silu"], b * WPB + W)
                        gate(eng, gslot, jp)
                        if W - 1 >= 0:
                            down_window(eng, W - 1)
                for es in range(161):
                    expert_gwins(t, es)
                down_window(t, WPB - 2)
                down_window(t, WPB - 1)

                t.wait_ge(sem["s_act_silu"], (b + 1) * WPB)
                t.wait_ge(sem["s_dve_evac"], 2 * (b + 1) * WPB)
                t.wait_ge(sem["s_dve_g"], (b + 1) * NT)
                t.wait_ge(sem["s_wdma"], wdma_qk_ready(b))
                for idx in range(32):
                    ec, tc = idx // 4, idx % 4
                    if b * 32 + idx > 1:
                        t.wait_ge(sem["s_act_qk"], b * 32 + idx - 1)
                    for c in range(KCH):
                        mm = t.matmul(kt_ps[idx % 2],
                                      wkT_sb[:, c * 1024 + ec * 128: c * 1024 + (ec + 1) * 128],
                                      xnT[:, c * 2048 + tc * 512: c * 2048 + (tc + 1) * 512],
                                      start=(c == 0), stop=(c == KCH - 1))
                    mm.then_inc(sem["s_pe_kt"], 1)
                def emit_qTt_scores(qt):
                    if DEBUG and qt == 1:
                        t.wait_ge(s_dbg, 112 * b + 96)
                    t.wait_ge(sem["s_act_exp"], b * NT + qt)
                    t.wait_ge(sem["s_act_qt"], b * NT + qt)
                    for ec in range(KCH):
                        for c in range(KCH):
                            mm = t.matmul(qt_ps[:, ec * 128:(ec + 1) * 128],
                                          wqT_sb[:, c * 1024 + ec * 128: c * 1024 + (ec + 1) * 128],
                                          xnT[:, c * 2048 + qt * 128: c * 2048 + (qt + 1) * 128],
                                          start=(c == 0), stop=(c == KCH - 1))
                    mm.then_inc(sem["s_pe_qt"], 1)
                    t.wait_ge(sem["s_act_qt"], b * NT + qt + 1)
                    if qt == 0:
                        t.wait_ge(sem["s_act_qk"], (b + 1) * 32)
                    for kc in range(4):
                        for ec in range(KCH):
                            mm = t.matmul(sc_ps[:, kc * 512:(kc + 1) * 512],
                                          qTt[:, ec * 128:(ec + 1) * 128],
                                          kT[:, ec * 2048 + kc * 512: ec * 2048 + (kc + 1) * 512],
                                          start=(ec == 0), stop=(ec == KCH - 1))
                    mm.then_inc(sem["s_pe_sc"], 1)

                def emit_pt_av(qt):
                    t.wait_ge(sem["s_dve_pb"], b * NT + qt + 1)
                    for h in range(2):
                        t.wait_ge(sem["s_dve_pt"], 2 * (b * NT + qt) + h)
                        for kt_i in range(8 * h, 8 * h + 8):
                            mm = t.matmul(ptf[:, (kt_i % 8) * 128:(kt_i % 8 + 1) * 128],
                                          Pb[:, kt_i * 128:(kt_i + 1) * 128], ident[:],
                                          start=True, stop=True)
                        mm.then_inc(sem["s_pe_pt"], 1)
                    t.wait_ge(sem["s_dve_pt"], 2 * (b * NT + qt + 1))
                    t.wait_ge(sem["s_dve_epi"], b * NT + qt)
                    for dh in range(2):
                        for kt_i in range(NT):
                            mm = t.matmul(s_ps[:, dh * 512:(dh + 1) * 512],
                                          PTs[:, kt_i * 128:(kt_i + 1) * 128],
                                          xnat[:, kt_i * 1024 + dh * 512: kt_i * 1024 + dh * 512 + 512],
                                          start=(kt_i == 0), stop=(kt_i == NT - 1))
                    mm.then_inc(sem["s_pe_av"], 1)

                for qt in range(NT):
                    emit_qTt_scores(qt)
                    if qt >= 1:
                        emit_pt_av(qt - 1)
                emit_pt_av(NT - 1)

        @blk.scalar
        def _(s):
            s.wait_ge(sem["s_cst"], 64)
            for b in range(NB):
                for ti in range(NT):
                    if ti == 0 and b > 0:
                        s.wait_ge(sem["s_dve_pb"], b * NT)
                    s.wait_ge(sem["s_xL"], 16 * (b * NT + ti + 1))
                    s.activation(ttmp[:], xin[ti % 2][:], AF.Square,
                                 accum_out=scr["s2"][:]).then_inc(sem["s_act_sq"], 1)
                    s.wait_ge(sem["s_dve_ln1"], b * NT + ti + 1)
                    s.activation(scr["sqv"][:], scr["varv"][:], AF.Sqrt, bias=scr["meanrow"][:]
                                 ).then_inc(sem["s_act_rstd"], 1)
                for ti in range(NT):
                    s.wait_ge(sem["s_dve_lg"], b * NT + ti + 1)
                    s.activation(probs[:], probs_pre[:], AF.Sigmoid
                                 ).then_inc(sem["s_act_sig"], 1)

                h1_3d = h1_t.ap().rearrange("p (i z) -> p i z", i=ICH)
                h1w_3d = [h1w[par].ap().rearrange("p (i z) -> p i z", i=ICH) for par in range(4)]
                GPB = 161 * 4
                for w in range(WPB):
                    hpar = w % 2
                    s.wait_ge(sem["s_pe_h1"], b * GPB + w // 2 + 1)
                    war = 2 * (b * WPB + w) - 6
                    if war > 0:
                        s.wait_ge(sem["s_pe_dn"], war)
                    s.activation(h1w_3d[w % 4], h1_3d[:, :, hpar * 256:(hpar + 1) * 256],
                                 AF.Silu).then_inc(sem["s_act_silu"], 1)

                s.wait_ge(sem["s_dve_evac"], 2 * (b + 1) * WPB)
                for ti in range(NT):
                    if b * NT + ti > 1:
                        s.wait_ge(sem["s_dve_g"], b * NT + ti - 1)
                    s.activation(gb_ps[ti % 2], routed[:, ti * 1024:(ti + 1) * 1024], AF.Silu
                                 ).then_inc(sem["s_act_g"], 1)

                for idx in range(32):
                    ec, tc = idx // 4, idx % 4
                    s.wait_ge(sem["s_pe_kt"], b * 32 + idx + 1)
                    s.add(kT[:, ec * 2048 + tc * 512: ec * 2048 + (tc + 1) * 512],
                          kt_ps[idx % 2], bk_sb[:, ec:ec + 1]).then_inc(sem["s_act_qk"], 1)
                for qt in range(NT):
                    s.wait_ge(sem["s_pe_qt"], b * NT + qt + 1)
                    for ec in range(KCH):
                        ins = s.add(qTt[:, ec * 128:(ec + 1) * 128],
                                    qt_ps[:, ec * 128:(ec + 1) * 128], bq_sb[:, ec:ec + 1])
                    ins.then_inc(sem["s_act_qt"], 1)
                    if DEBUG and qt == 1:
                        s.wait_ge(s_dbg, 112 * b + 112)
                    s.wait_ge(sem["s_pe_sc"], b * NT + qt + 1)
                    s.wait_ge(sem["s_dve_sm1"], b * NT + qt + 1)
                    s.activation(P[:], sc_ps, AF.Exp, bias=scr["negmax"][:], scale=scr["inv32"][:],
                                 accum_out=scr["rsumv"][:]).then_inc(sem["s_act_exp"], 1)

        @blk.vector
        def _(v):
            v.memset(scr["inv32"][:], 1.0 / 32.0)
            v.memset(scr["meanrow"][:], 1e-5)
            v.wait_ge(sem["s_cst"], 64)
            for b in range(NB):
                for ti in range(NT):
                    xb = xin[ti % 2]
                    v.wait_ge(sem["s_xL"], 16 * (b * NT + ti + 1))
                    v.reduce_sum(scr["s1"][:], xb[:], axis=AX)
                    v.drain()
                    v.tensor_scalar_mul(scr["meanv"][:], scr["s1"][:], 1.0 / D)
                    v.drain()
                    v.tensor_mul(scr["msq"][:], scr["meanv"][:], scr["meanv"][:])
                    v.drain()
                    v.wait_ge(sem["s_act_sq"], b * NT + ti + 1)
                    v.scalar_tensor_tensor(out=scr["varv"][:], in0=scr["s2"][:],
                                           scalar=1.0 / D, in1=scr["msq"][:],
                                           op0=ALU.mult, op1=ALU.subtract
                                           ).then_inc(sem["s_dve_ln1"], 1)
                    v.wait_ge(sem["s_act_rstd"], b * NT + ti + 1)
                    v.reciprocal(scr["rstdv"][:], scr["sqv"][:])
                    v.drain()
                    v.tensor_scalar(xnat[:, ti * 1024:(ti + 1) * 1024], xb[:],
                                    scr["meanv"][:], scr["rstdv"][:],
                                    op0=ALU.subtract, op1=ALU.mult).then_inc(sem["s_dve_xn"], 1)
                    v.wait_ge(sem["s_pe_ln"], b * NT + ti + 1)
                    dst = xnT.ap().rearrange("p (c z) -> p c z", c=KCH)[:, :, ti * 128:(ti + 1) * 128]
                    src = ptf.ap().rearrange("p (c z) -> p c z", c=KCH)
                    v.tensor_copy(dst, src).then_inc(sem["s_dve_xnT"], 1)
                for ti in range(NT):
                    v.wait_ge(sem["s_pe_lg"], b * NT + ti + 1)
                    v.tensor_add(probs_pre[:], lg_ps, bias_sb[:]).then_inc(sem["s_dve_lg"], 1)
                    v.wait_ge(sem["s_act_sig"], b * NT + ti + 1)
                    v.max(out=top8[:], in_=probs[:])
                    v.drain()
                    v.reduce_sum(scr["tsumv"][:], top8[:], axis=AX)
                    v.drain()
                    v.tensor_scalar_add(scr["tsumv"][:], scr["tsumv"][:], 1e-9)
                    v.drain()
                    v.reciprocal(scr["tinvv"][:], scr["tsumv"][:])
                    v.drain()
                    v.tensor_scalar_mul(scr["tinvv"][:], scr["tinvv"][:], 2.5)
                    v.drain()
                    v.tensor_scalar(ge[:], probs[:], top8[:, 7:8], None, op0=ALU.is_ge)
                    v.drain()
                    ct = comb[:, ti * 160:(ti + 1) * 160]
                    v.tensor_mul(ct, ge[:], probs[:])
                    v.drain()
                    v.tensor_scalar(ct, ct, scr["tinvv"][:], None, op0=ALU.mult)
                    v.drain()

                for j in range(NW):
                    for m in range(2):
                        w = j
                        v.wait_ge(sem["s_pe_dn"], dn_cnt(b, w, m))
                        v.tensor_copy(routed[:, (2 * j + m) * 1024:(2 * j + m + 1) * 1024],
                                      dn_ps[m]).then_inc(sem["s_dve_evac"], 1)

                def evac_expert(eng, i_or_e, sub):
                    for j in range(NW):
                        for m in range(2):
                            if UNROLL_EXPERTS:
                                e = i_or_e
                            else:
                                e = 2 * i_or_e + sub
                            w = (e + 1) * NW + j
                            eng.wait_ge(sem["s_pe_dn"], dn_cnt(b, w, m))
                            rt = routed[:, (2 * j + m) * 1024:(2 * j + m + 1) * 1024]
                            if UNROLL_EXPERTS:
                                cap = comb[:, (2 * j + m) * 160 + e: (2 * j + m) * 160 + e + 1]
                            else:
                                cap = comb[:, bass.ts((2 * j + m) * 160 + e, 1)]
                            eng.scalar_tensor_tensor(out=rt, in0=dn_ps[m], scalar=cap, in1=rt,
                                                     op0=ALU.mult, op1=ALU.add
                                                     ).then_inc(sem["s_dve_evac"], 1)
                if UNROLL_EXPERTS:
                    for e in range(E):
                        evac_expert(v, e, 0)
                else:
                    with v.Fori(0, E // 2) as i:
                        evac_expert(v, i, 0)
                        evac_expert(v, i, 1)

                for ti in range(NT):
                    v.wait_ge(sem["s_act_g"], b * NT + ti + 1)
                    v.tensor_copy(routed[:, ti * 1024:(ti + 1) * 1024], gb_ps[ti % 2]
                                  ).then_inc(sem["s_dve_g"], 1)

                for qt in range(NT):
                    v.wait_ge(sem["s_pe_sc"], b * NT + qt + 1)
                    v.reduce_max(scr["rmaxv"][:], sc_ps, axis=AX)
                    v.drain()
                    v.tensor_scalar_mul(scr["negmax"][:], scr["rmaxv"][:], -1.0 / 32.0)
                    v.drain().then_inc(sem["s_dve_sm1"], 1)
                    v.wait_ge(sem["s_act_exp"], b * NT + qt + 1)
                    v.reciprocal(scr["zinv"][:], scr["rsumv"][:])
                    v.drain()
                    v.tensor_scalar(Pb[:], P[:], scr["zinv"][:], 2.0 / S,
                                    op0=ALU.mult, op1=ALU.subtract)
                    v.drain()
                    v.tensor_add(Pb[:, qt * 128:(qt + 1) * 128],
                                 Pb[:, qt * 128:(qt + 1) * 128], ident[:])
                    v.drain().then_inc(sem["s_dve_pb"], 1)
                    for h in range(2):
                        v.wait_ge(sem["s_pe_pt"], 2 * (b * NT + qt) + h + 1)
                        v.tensor_copy(PTs[:, h * 1024:(h + 1) * 1024], ptf[:]
                                      ).then_inc(sem["s_dve_pt"], 1)
                    v.wait_ge(sem["s_pe_av"], b * NT + qt + 1)
                    v.wait_ge(sem["s_xA"], 16 * (b * NT + qt + 1))
                    rt = routed[:, qt * 1024:(qt + 1) * 1024]
                    v.tensor_mul(rt, s_ps, rt)
                    v.drain()
                    v.tensor_add(xin[qt % 2][:], rt, xin[qt % 2][:]).then_inc(sem["s_dve_epi"], 1)

    return nc


def kernel(L0_attn_x, L0_attn_Wq, L0_attn_bq, L0_attn_Wk, L0_attn_bk,
           L0_ffn_router_w, L0_ffn_expert_bias, L0_ffn_w_gate, L0_ffn_w_down,
           L0_ffn_w_gate_shared, L0_ffn_w_down_shared):
    def pmajor(a, nch):
        C = a.shape[0] // 128
        return np.ascontiguousarray(
            a.reshape(C, 128, -1).transpose(1, 0, 2).reshape(128, -1))

    x = np.ascontiguousarray(np.asarray(L0_attn_x, dtype=np.float32))
    wqT = pmajor(np.asarray(L0_attn_Wq, dtype=np.float32).T, KCH)
    wkT = pmajor(np.asarray(L0_attn_Wk, dtype=np.float32).T, KCH)
    bqr = np.ascontiguousarray(np.asarray(L0_attn_bq, dtype=np.float32).reshape(8, 128).T)
    bkr = np.ascontiguousarray(np.asarray(L0_attn_bk, dtype=np.float32).reshape(8, 128).T)
    rwT = pmajor(np.asarray(L0_ffn_router_w, dtype=np.float32).T, KCH)
    biasbc = np.ascontiguousarray(
        np.broadcast_to(np.asarray(L0_ffn_expert_bias, dtype=np.float32), (128, E)))
    wgT3 = np.asarray(L0_ffn_w_gate, dtype=np.float32).transpose(0, 2, 1)
    wgT = np.ascontiguousarray(
        wgT3.reshape(E, KCH, 128, INTER).transpose(0, 2, 1, 3).reshape(E, 128, 4096))
    wdT3 = np.asarray(L0_ffn_w_down, dtype=np.float32).transpose(0, 2, 1)
    wdT = np.ascontiguousarray(
        wdT3.reshape(E, ICH, 128, D).transpose(0, 2, 1, 3).reshape(E, 128, 4096))
    wgsT = pmajor(np.asarray(L0_ffn_w_gate_shared, dtype=np.float32).T, KCH)
    wdsT = pmajor(np.asarray(L0_ffn_w_down_shared, dtype=np.float32).T, ICH)
    identh = np.eye(128, dtype=np.float32).astype(BF)

    nc = build_program()
    in_maps = []
    for c in range(8):
        in_maps.append(dict(x=np.ascontiguousarray(x[2 * c:2 * c + 2]),
                            wqT=wqT, wkT=wkT, bqr=bqr, bkr=bkr, rwT=rwT,
                            biasbc=biasbc, wgT=wgT, wdT=wdT, wgsT=wgsT, wdsT=wdsT,
                            ident=identh))
    res = run_bass_kernel_spmd(nc, in_maps, core_ids=list(range(8)))
    return np.concatenate([np.asarray(r["y"]) for r in res.results], axis=0)


# version 12 (final)
# speedup vs baseline: 1.0886x; 1.0670x over previous
import numpy as np
import ml_dtypes
import concourse.bass as bass
import concourse.mybir as mybir
from concourse.bass_utils import run_bass_kernel_spmd

F32 = mybir.dt.float32
BF16 = mybir.dt.bfloat16
AF = mybir.ActivationFunctionType
ALU = mybir.AluOpType
AX = mybir.AxisListType.X
BF = ml_dtypes.bfloat16

NB = 2
S = 2048
D = 1024
E = 160
INTER = 512
NT = 16
NW = 8
KCH = 8
ICH = 4
WPB = 161 * NW
UNROLL_EXPERTS = True
DEBUG = False

SB0 = 16512
OFF_XNAT = SB0 + 0
OFF_ROUT = SB0 + 32768
OFF_XIN = SB0 + 98304
OFF_SCR = SB0 + 106496
OFF_XNT = SB0 + 110592
OFF_SLOTW = SB0 + 143360
OFF_WQT = SB0 + 176128
OFF_SLOTC = SB0 + 192512
OFF_QTT = SB0 + 208896


def build_program():
    nc = bass.Bass("TRN2", target_bir_lowering=False)

    x_d = nc.dram_tensor("x", [NB, S, D], F32, kind="ExternalInput").ap()
    y_d = nc.dram_tensor("y", [NB, S, D], F32, kind="ExternalOutput").ap()
    wqT_d = nc.dram_tensor("wqT", [128, 8192], F32, kind="ExternalInput").ap()
    wkT_d = nc.dram_tensor("wkT", [128, 8192], F32, kind="ExternalInput").ap()
    bq_d = nc.dram_tensor("bqr", [128, 8], F32, kind="ExternalInput").ap()
    bk_d = nc.dram_tensor("bkr", [128, 8], F32, kind="ExternalInput").ap()
    rwT_d = nc.dram_tensor("rwT", [128, 1280], F32, kind="ExternalInput").ap()
    bias_d = nc.dram_tensor("biasbc", [128, E], F32, kind="ExternalInput").ap()
    wg_d = nc.dram_tensor("wgT", [E, 128, 4096], F32, kind="ExternalInput").ap()
    wd_d = nc.dram_tensor("wdT", [E, 128, 4096], F32, kind="ExternalInput").ap()
    wgs_d = nc.dram_tensor("wgsT", [128, 4096], F32, kind="ExternalInput").ap()
    wds_d = nc.dram_tensor("wdsT", [128, 4096], F32, kind="ExternalInput").ap()
    id_d = nc.dram_tensor("ident", [128, 128], BF16, kind="ExternalInput").ap()
    if DEBUG:
        d_xn = nc.dram_tensor("d_xn", [NB, 128, 16384], BF16, kind="ExternalOutput").ap()
        d_xnT = nc.dram_tensor("d_xnT", [NB, 128, 16384], BF16, kind="ExternalOutput").ap()
        d_comb = nc.dram_tensor("d_comb", [NB, 128, 2560], F32, kind="ExternalOutput").ap()
        d_g = nc.dram_tensor("d_g", [NB, 128, 16384], F32, kind="ExternalOutput").ap()
        d_kT = nc.dram_tensor("d_kT", [NB, 128, 16384], BF16, kind="ExternalOutput").ap()
        d_qt0 = nc.dram_tensor("d_qt0", [NB, 128, 1024], BF16, kind="ExternalOutput").ap()
        d_P0 = nc.dram_tensor("d_P0", [NB, 128, 2048], F32, kind="ExternalOutput").ap()
        s_dbg = nc.alloc_semaphore("s_dbg")

    A_ = nc.alloc_sbuf_tensor_at
    xnat = A_("xnat", [128, 16384], BF16, offset=OFF_XNAT)
    routed = A_("routed", [128, 16384], F32, offset=OFF_ROUT)
    xin = [A_(f"xin{i}", [128, 1024], F32, offset=OFF_XIN + i * 4096) for i in range(2)]
    qTt = A_("qTt", [128, 1024], BF16, offset=OFF_SCR)
    bias_sb = A_("bias_sb", [128, 160], F32, offset=OFF_SCR + 2048)
    bq_sb = A_("bq_sb", [128, 8], F32, offset=OFF_SCR + 2688)
    bk_sb = A_("bk_sb", [128, 8], F32, offset=OFF_SCR + 2720)
    ident = A_("ident_sb", [128, 128], BF16, offset=OFF_SCR + 2752)
    sc_base = OFF_SCR + 3008
    names = ["s1", "s2", "meanv", "varv", "msq", "rstdv", "sqv", "rmaxv",
             "negmax", "rsumv", "zinv", "meanrow", "tsumv", "tinvv", "inv32"]
    scr = {}
    for i, n in enumerate(names):
        scr[n] = A_(n, [128, 1], F32, offset=sc_base + 32 * i)
    top8 = A_("top8", [128, 8], F32, offset=sc_base + 32 * len(names))
    xnT = A_("xnT", [128, 16384], BF16, offset=OFF_XNT)
    wg_sb = [A_(f"wg{i}", [128, 4096], BF16, offset=OFF_SLOTW + i * 16384) for i in range(2)]
    wd_sb = [A_(f"wd{i}", [128, 4096], BF16, offset=OFF_SLOTW + 8192 + i * 16384) for i in range(2)]
    kT = A_("kT", [128, 16384], BF16, offset=OFF_SLOTW)
    wqT_sb = A_("wqT_sb", [128, 8192], BF16, offset=OFF_WQT)
    comb = A_("comb", [128, 2560], F32, offset=OFF_SLOTC)
    h1w_offs = [OFF_SLOTC + 10240, OFF_SLOTC + 12288, OFF_SLOTC + 14336, OFF_QTT]
    h1w = [A_(f"h1w{i}", [128, 1024], BF16, offset=h1w_offs[i]) for i in range(4)]
    rwT_sb = A_("rwT_sb", [128, 1280], BF16, offset=OFF_SLOTC + 10240)
    ttmp = A_("ttmp", [128, 1024], F32, offset=OFF_SLOTC)
    probs_pre = A_("probs_pre", [128, 160], F32, offset=SB0 + 210944)
    probs = A_("probs", [128, 160], F32, offset=SB0 + 211584)
    ge = A_("ge", [128, 160], F32, offset=SB0 + 210944)
    wkT_sb = A_("wkT_sb", [128, 8192], BF16, offset=OFF_SLOTC)
    P = A_("P", [128, 2048], F32, offset=OFF_SLOTC)
    Pb = A_("Pb", [128, 2048], BF16, offset=OFF_SLOTC + 8192)
    PTs = A_("PTs", [128, 2048], BF16, offset=OFF_SLOTC + 12288)

    h1_t = nc.alloc_psum_tensor("h1t", [128, 2048], F32)
    dn0_t = nc.alloc_psum_tensor("dn0t", [128, 1024], F32)
    dn1_t = nc.alloc_psum_tensor("dn1t", [128, 1024], F32)
    nc.psum_base = 0
    arena = nc.alloc_psum_tensor("arena", [128, 3072], F32)
    ptb = nc.alloc_psum_tensor("ptb", [128, 2048], BF16)
    nc.psum_base = 0
    _ph0 = nc.alloc_psum_tensor("ph0", [128, 3072], F32)
    ptf = nc.alloc_psum_tensor("ptf", [128, 1024], F32)
    h1_ps = h1_t[:]
    dn_ps = [dn0_t[:], dn1_t[:]]
    lg_ps = arena[:, 0:160]
    qt_ps = arena[:, 0:1024]
    sc_ps = arena[:, 0:2048]
    s_ps = arena[:, 2048:3072]
    kt_ps = [arena[:, 2048:2560], arena[:, 2560:3072]]
    gb_ps = [h1_t[:, 0:1024], h1_t[:, 1024:2048]]

    sem_names = ["s_wdma", "s_xL", "s_xA", "s_out", "s_cst",
                 "s_pe_ln", "s_pe_lg", "s_pe_h1", "s_pe_dn",
                 "s_pe_qt", "s_pe_kt", "s_pe_sc", "s_pe_pt", "s_pe_av",
                 "s_act_rstd", "s_act_sig", "s_act_silu", "s_act_g",
                 "s_act_exp", "s_act_qk", "s_act_qt", "s_act_sq",
                 "s_dve_ln1", "s_dve_xn", "s_dve_xnT", "s_dve_lg",
                 "s_dve_sm1", "s_dve_pb", "s_dve_pt", "s_dve_epi",
                 "s_dve_evac", "s_dve_g"]
    sem = {n: nc.alloc_semaphore(n) for n in sem_names}

    def wdma_rwt_ready(b):
        return 16 * (b * 325 + 1)

    def wdma_shared_ready(b):
        return 16 * (b * 325 + 3)

    def wdma_exp_ready(b, e):
        return 16 * (b * 325 + 2 * e + 5)

    def wdma_qk_ready(b):
        return 16 * (325 * (b + 1))

    def dn_cnt(b, w, m):
        return (b * WPB + w) * 2 + m + 1

    def evac_cnt(b, w, m):
        return (b * WPB + w) * 2 + m + 1

    def silu_cnt(b, w):
        return b * WPB + w + 1


    with nc.Block() as blk:

        @blk.sync
        def _(sy):
            sy.dma_start(out=ident[:], in_=id_d).then_inc(sem["s_cst"], 16)
            sy.dma_start(out=bias_sb[:], in_=bias_d).then_inc(sem["s_cst"], 16)
            sy.dma_start(out=bq_sb[:], in_=bq_d).then_inc(sem["s_cst"], 16)
            sy.dma_start(out=bk_sb[:], in_=bk_d).then_inc(sem["s_cst"], 16)
            for b in range(NB):
                if b > 0:
                    sy.wait_ge(sem["s_out"], 16 * b * NT)
                for t in range(NT):
                    war = b * NT + t - 1
                    if war > 0:
                        sy.wait_ge(sem["s_dve_xn"], war)
                        sy.wait_ge(sem["s_act_sq"], war)
                    sy.dma_start(out=xin[t % 2][:], in_=x_d[b, t * 128:(t + 1) * 128, :]
                                 ).then_inc(sem["s_xL"], 16)
                if DEBUG:
                    sy.wait_ge(sem["s_dve_xnT"], (b + 1) * NT)
                    sy.dma_start(out=d_xn[b], in_=xnat[:]).then_inc(s_dbg, 16)
                    sy.dma_start(out=d_xnT[b], in_=xnT[:]).then_inc(s_dbg, 16)
                    sy.wait_ge(sem["s_dve_evac"], b * 2 * WPB + 17)
                    sy.dma_start(out=d_comb[b], in_=comb[:]).then_inc(s_dbg, 16)
                    sy.wait_ge(sem["s_dve_g"], (b + 1) * NT)
                    sy.dma_start(out=d_g[b], in_=routed[:]).then_inc(s_dbg, 16)
                    sy.wait_ge(sem["s_act_qk"], (b + 1) * 32)
                    sy.dma_start(out=d_kT[b], in_=kT[:]).then_inc(s_dbg, 16)
                    sy.wait_ge(sem["s_act_qt"], b * NT + 1)
                    sy.dma_start(out=d_qt0[b], in_=qTt[:]).then_inc(s_dbg, 16)
                    sy.wait_ge(sem["s_dve_pb"], b * NT + 1)
                    sy.dma_start(out=d_P0[b], in_=P[:]).then_inc(s_dbg, 16)
                sy.wait_ge(sem["s_dve_xn"], (b + 1) * NT)
                for qt in range(NT):
                    war = b * NT + qt - 1
                    if war > 0:
                        sy.wait_ge(sem["s_dve_epi"], war)
                        sy.wait_ge(sem["s_out"], 16 * war)
                    sy.dma_start(out=xin[qt % 2][:], in_=x_d[b, qt * 128:(qt + 1) * 128, :]
                                 ).then_inc(sem["s_xA"], 16)
                    sy.wait_ge(sem["s_dve_epi"], b * NT + qt + 1)
                    sy.dma_start(out=y_d[b, qt * 128:(qt + 1) * 128, :], in_=xin[qt % 2][:]
                                 ).then_inc(sem["s_out"], 16)
            sy.wait_ge(sem["s_out"], 16 * NB * NT)

        @blk.gpsimd
        def _(g):
            for b in range(NB):
                if b > 0:
                    g.wait_ge(sem["s_dve_pt"], 2 * b * NT)
                g.dma_start(out=rwT_sb[:], in_=rwT_d).then_inc(sem["s_wdma"], 16)
                if b > 0:
                    g.wait_ge(sem["s_pe_sc"], b * NT)
                g.dma_start(out=wg_sb[1][:], in_=wgs_d).then_inc(sem["s_wdma"], 16)
                g.dma_start(out=wd_sb[1][:], in_=wds_d).then_inc(sem["s_wdma"], 16)
                for e in range(E):
                    if e >= 2:
                        g.wait_ge(sem["s_pe_dn"], 2 * (b * WPB + e * NW))
                    elif e == 1:
                        g.wait_ge(sem["s_pe_dn"], 2 * (b * WPB + NW))
                    elif b > 0:
                        g.wait_ge(sem["s_pe_sc"], b * NT)
                    g.dma_start(out=wg_sb[e % 2][:], in_=wg_d[e]).then_inc(sem["s_wdma"], 16)
                    g.dma_start(out=wd_sb[e % 2][:], in_=wd_d[e]).then_inc(sem["s_wdma"], 16)
                g.wait_ge(sem["s_pe_dn"], 2 * (b + 1) * WPB)
                g.wait_ge(sem["s_dve_evac"], 2 * (b + 1) * WPB)
                g.dma_start(out=wqT_sb[:], in_=wqT_d).then_inc(sem["s_wdma"], 16)
                g.dma_start(out=wkT_sb[:], in_=wkT_d).then_inc(sem["s_wdma"], 16)

        @blk.tensor
        def _(t):
            t.wait_ge(sem["s_cst"], 64)
            for b in range(NB):
                if b > 0:
                    t.wait_ge(sem["s_act_exp"], b * NT)
                    t.wait_ge(sem["s_dve_epi"], b * NT)
                    t.wait_ge(sem["s_dve_pt"], 2 * b * NT)
                    t.wait_ge(sem["s_dve_g"], b * NT)
                for ti in range(NT):
                    t.wait_ge(sem["s_dve_xn"], b * NT + ti + 1)
                    if b * NT + ti > 0:
                        t.wait_ge(sem["s_dve_xnT"], b * NT + ti)
                    for c in range(KCH):
                        mm = t.matmul(ptf[:, c * 128:(c + 1) * 128],
                                      xnat[:, ti * 1024 + c * 128: ti * 1024 + (c + 1) * 128],
                                      ident[:], start=True, stop=True)
                    mm.then_inc(sem["s_pe_ln"], 1)
                t.wait_ge(sem["s_wdma"], wdma_rwt_ready(b))
                for ti in range(NT):
                    t.wait_ge(sem["s_dve_xnT"], b * NT + ti + 1)
                    if b * NT + ti > 0:
                        t.wait_ge(sem["s_dve_lg"], b * NT + ti)
                    for c in range(KCH):
                        mm = t.matmul(lg_ps, xnT[:, c * 2048 + ti * 128: c * 2048 + (ti + 1) * 128],
                                      rwT_sb[:, c * 160:(c + 1) * 160],
                                      start=(c == 0), stop=(c == KCH - 1))
                    mm.then_inc(sem["s_pe_lg"], 1)
                t.wait_ge(sem["s_dve_lg"], (b + 1) * NT)
                t.wait_ge(sem["s_wdma"], wdma_shared_ready(b))

                def gate(eng, wslot, jp):
                    for ich in range(ICH):
                        for c in range(KCH):
                            mm = eng.matmul(
                                h1_ps[:, ich * 512:(ich + 1) * 512],
                                wg_sb[wslot][:, c * 512 + ich * 128: c * 512 + (ich + 1) * 128],
                                xnT[:, c * 2048 + jp * 512: c * 2048 + (jp + 1) * 512],
                                start=(c == 0), stop=(c == KCH - 1))
                    mm.then_inc(sem["s_pe_h1"], 1)

                def down(eng, wslot, par, evac_base):
                    for m in range(2):
                        if evac_base + m > 0:
                            eng.wait_ge(sem["s_dve_evac"], evac_base + m)
                        for ich in range(ICH):
                            for half in range(2):
                                mm = eng.matmul(
                                    dn_ps[m][:, half * 512:(half + 1) * 512],
                                    h1w[par][:, ich * 256 + m * 128: ich * 256 + m * 128 + 128],
                                    wd_sb[wslot][:, ich * 1024 + half * 512: ich * 1024 + (half + 1) * 512],
                                    start=(ich == 0), stop=(ich == ICH - 1))
                        mm.then_inc(sem["s_pe_dn"], 1)

                def down_window(eng, wl):
                    wl_e = wl // NW - 1
                    dslot = 1 if wl_e < 0 else wl_e % 2
                    eng.wait_ge(sem["s_act_silu"], b * WPB + wl + 1)
                    down(eng, dslot, wl % 4, 2 * (b * WPB + wl) - 1)

                def expert_gwins(eng, es):
                    e = es - 1
                    gslot = 1 if e < 0 else e % 2
                    if e >= 0:
                        eng.wait_ge(sem["s_wdma"], wdma_exp_ready(b, e))
                    for jp in range(4):
                        W = es * NW + 2 * jp
                        if W - 2 >= 0:
                            down_window(eng, W - 2)
                        if W > 0:
                            eng.wait_ge(sem["s_act_silu"], b * WPB + W)
                        gate(eng, gslot, jp)
                        if W - 1 >= 0:
                            down_window(eng, W - 1)
                for es in range(161):
                    expert_gwins(t, es)
                down_window(t, WPB - 2)
                down_window(t, WPB - 1)

                t.wait_ge(sem["s_act_silu"], (b + 1) * WPB)
                t.wait_ge(sem["s_dve_evac"], 2 * (b + 1) * WPB)
                t.wait_ge(sem["s_dve_g"], (b + 1) * NT)
                t.wait_ge(sem["s_wdma"], wdma_qk_ready(b))
                for idx in range(32):
                    ec, tc = idx // 4, idx % 4
                    if b * 32 + idx > 1:
                        t.wait_ge(sem["s_act_qk"], b * 32 + idx - 1)
                    for c in range(KCH):
                        mm = t.matmul(kt_ps[idx % 2],
                                      wkT_sb[:, c * 1024 + ec * 128: c * 1024 + (ec + 1) * 128],
                                      xnT[:, c * 2048 + tc * 512: c * 2048 + (tc + 1) * 512],
                                      start=(c == 0), stop=(c == KCH - 1))
                    mm.then_inc(sem["s_pe_kt"], 1)
                def emit_qTt_scores(qt):
                    if DEBUG and qt == 1:
                        t.wait_ge(s_dbg, 112 * b + 96)
                    t.wait_ge(sem["s_act_exp"], b * NT + qt)
                    t.wait_ge(sem["s_act_qt"], b * NT + qt)
                    for ec in range(KCH):
                        for c in range(KCH):
                            mm = t.matmul(qt_ps[:, ec * 128:(ec + 1) * 128],
                                          wqT_sb[:, c * 1024 + ec * 128: c * 1024 + (ec + 1) * 128],
                                          xnT[:, c * 2048 + qt * 128: c * 2048 + (qt + 1) * 128],
                                          start=(c == 0), stop=(c == KCH - 1))
                    mm.then_inc(sem["s_pe_qt"], 1)
                    t.wait_ge(sem["s_act_qt"], b * NT + qt + 1)
                    if qt == 0:
                        t.wait_ge(sem["s_act_qk"], (b + 1) * 32)
                    for kc in range(4):
                        for ec in range(KCH):
                            mm = t.matmul(sc_ps[:, kc * 512:(kc + 1) * 512],
                                          qTt[:, ec * 128:(ec + 1) * 128],
                                          kT[:, ec * 2048 + kc * 512: ec * 2048 + (kc + 1) * 512],
                                          start=(ec == 0), stop=(ec == KCH - 1))
                    mm.then_inc(sem["s_pe_sc"], 1)

                def emit_pt_av(qt):
                    t.wait_ge(sem["s_dve_pb"], b * NT + qt + 1)
                    for h in range(2):
                        t.wait_ge(sem["s_dve_pt"], 2 * (b * NT + qt) + h)
                        for kt_i in range(8 * h, 8 * h + 8):
                            mm = t.matmul(ptf[:, (kt_i % 8) * 128:(kt_i % 8 + 1) * 128],
                                          Pb[:, kt_i * 128:(kt_i + 1) * 128], ident[:],
                                          start=True, stop=True)
                        mm.then_inc(sem["s_pe_pt"], 1)
                    t.wait_ge(sem["s_dve_epi"], b * NT + qt)
                    for dh in range(2):
                        for kt_i in range(NT):
                            if dh == 0 and kt_i in (0, 8):
                                t.wait_ge(sem["s_dve_pt"],
                                          2 * (b * NT + qt) + kt_i // 8 + 1)
                            mm = t.matmul(s_ps[:, dh * 512:(dh + 1) * 512],
                                          PTs[:, kt_i * 128:(kt_i + 1) * 128],
                                          xnat[:, kt_i * 1024 + dh * 512: kt_i * 1024 + dh * 512 + 512],
                                          start=(kt_i == 0), stop=(kt_i == NT - 1))
                    mm.then_inc(sem["s_pe_av"], 1)

                for qt in range(NT):
                    emit_qTt_scores(qt)
                    if qt >= 1:
                        emit_pt_av(qt - 1)
                emit_pt_av(NT - 1)

        @blk.scalar
        def _(s):
            s.wait_ge(sem["s_cst"], 64)
            for b in range(NB):
                for ti in range(NT):
                    if ti == 0 and b > 0:
                        s.wait_ge(sem["s_dve_pb"], b * NT)
                    s.wait_ge(sem["s_xL"], 16 * (b * NT + ti + 1))
                    s.activation(ttmp[:], xin[ti % 2][:], AF.Square,
                                 accum_out=scr["s2"][:]).then_inc(sem["s_act_sq"], 1)
                    s.wait_ge(sem["s_dve_ln1"], b * NT + ti + 1)
                    s.activation(scr["sqv"][:], scr["varv"][:], AF.Sqrt, bias=scr["meanrow"][:]
                                 ).then_inc(sem["s_act_rstd"], 1)
                for ti in range(NT):
                    s.wait_ge(sem["s_dve_lg"], b * NT + ti + 1)
                    s.activation(probs[:], probs_pre[:], AF.Sigmoid
                                 ).then_inc(sem["s_act_sig"], 1)

                h1_3d = h1_t.ap().rearrange("p (i z) -> p i z", i=ICH)
                h1w_3d = [h1w[par].ap().rearrange("p (i z) -> p i z", i=ICH) for par in range(4)]
                GPB = 161 * 4
                for w in range(WPB):
                    hpar = w % 2
                    s.wait_ge(sem["s_pe_h1"], b * GPB + w // 2 + 1)
                    war = 2 * (b * WPB + w) - 6
                    if war > 0:
                        s.wait_ge(sem["s_pe_dn"], war)
                    s.activation(h1w_3d[w % 4], h1_3d[:, :, hpar * 256:(hpar + 1) * 256],
                                 AF.Silu).then_inc(sem["s_act_silu"], 1)

                s.wait_ge(sem["s_dve_evac"], 2 * (b + 1) * WPB)
                for ti in range(NT):
                    if b * NT + ti > 1:
                        s.wait_ge(sem["s_dve_g"], b * NT + ti - 1)
                    s.activation(gb_ps[ti % 2], routed[:, ti * 1024:(ti + 1) * 1024], AF.Silu
                                 ).then_inc(sem["s_act_g"], 1)

                for idx in range(32):
                    ec, tc = idx // 4, idx % 4
                    s.wait_ge(sem["s_pe_kt"], b * 32 + idx + 1)
                    s.add(kT[:, ec * 2048 + tc * 512: ec * 2048 + (tc + 1) * 512],
                          kt_ps[idx % 2], bk_sb[:, ec:ec + 1]).then_inc(sem["s_act_qk"], 1)
                for qt in range(NT):
                    s.wait_ge(sem["s_pe_qt"], b * NT + qt + 1)
                    for ec in range(KCH):
                        ins = s.add(qTt[:, ec * 128:(ec + 1) * 128],
                                    qt_ps[:, ec * 128:(ec + 1) * 128], bq_sb[:, ec:ec + 1])
                    ins.then_inc(sem["s_act_qt"], 1)
                    if DEBUG and qt == 1:
                        s.wait_ge(s_dbg, 112 * b + 112)
                    s.wait_ge(sem["s_pe_sc"], b * NT + qt + 1)
                    s.wait_ge(sem["s_dve_sm1"], b * NT + qt + 1)
                    s.activation(P[:], sc_ps, AF.Exp, bias=scr["negmax"][:], scale=scr["inv32"][:],
                                 accum_out=scr["rsumv"][:]).then_inc(sem["s_act_exp"], 1)

        @blk.vector
        def _(v):
            v.memset(scr["inv32"][:], 1.0 / 32.0)
            v.memset(scr["meanrow"][:], 1e-5)
            v.wait_ge(sem["s_cst"], 64)
            for b in range(NB):
                for ti in range(NT):
                    xb = xin[ti % 2]
                    v.wait_ge(sem["s_xL"], 16 * (b * NT + ti + 1))
                    v.reduce_sum(scr["s1"][:], xb[:], axis=AX)
                    v.drain()
                    v.tensor_scalar_mul(scr["meanv"][:], scr["s1"][:], 1.0 / D)
                    v.drain()
                    v.tensor_mul(scr["msq"][:], scr["meanv"][:], scr["meanv"][:])
                    v.drain()
                    v.wait_ge(sem["s_act_sq"], b * NT + ti + 1)
                    v.scalar_tensor_tensor(out=scr["varv"][:], in0=scr["s2"][:],
                                           scalar=1.0 / D, in1=scr["msq"][:],
                                           op0=ALU.mult, op1=ALU.subtract
                                           ).then_inc(sem["s_dve_ln1"], 1)
                    v.wait_ge(sem["s_act_rstd"], b * NT + ti + 1)
                    v.reciprocal(scr["rstdv"][:], scr["sqv"][:])
                    v.drain()
                    v.tensor_scalar(xnat[:, ti * 1024:(ti + 1) * 1024], xb[:],
                                    scr["meanv"][:], scr["rstdv"][:],
                                    op0=ALU.subtract, op1=ALU.mult).then_inc(sem["s_dve_xn"], 1)
                    v.wait_ge(sem["s_pe_ln"], b * NT + ti + 1)
                    dst = xnT.ap().rearrange("p (c z) -> p c z", c=KCH)[:, :, ti * 128:(ti + 1) * 128]
                    src = ptf.ap().rearrange("p (c z) -> p c z", c=KCH)
                    v.tensor_copy(dst, src).then_inc(sem["s_dve_xnT"], 1)
                for ti in range(NT):
                    v.wait_ge(sem["s_pe_lg"], b * NT + ti + 1)
                    v.tensor_add(probs_pre[:], lg_ps, bias_sb[:]).then_inc(sem["s_dve_lg"], 1)
                    v.wait_ge(sem["s_act_sig"], b * NT + ti + 1)
                    v.max(out=top8[:], in_=probs[:])
                    v.drain()
                    v.reduce_sum(scr["tsumv"][:], top8[:], axis=AX)
                    v.drain()
                    v.tensor_scalar_add(scr["tsumv"][:], scr["tsumv"][:], 1e-9)
                    v.drain()
                    v.reciprocal(scr["tinvv"][:], scr["tsumv"][:])
                    v.drain()
                    v.tensor_scalar_mul(scr["tinvv"][:], scr["tinvv"][:], 2.5)
                    v.drain()
                    v.tensor_scalar(ge[:], probs[:], top8[:, 7:8], None, op0=ALU.is_ge)
                    v.drain()
                    ct = comb[:, ti * 160:(ti + 1) * 160]
                    v.tensor_mul(ct, ge[:], probs[:])
                    v.drain()
                    v.tensor_scalar(ct, ct, scr["tinvv"][:], None, op0=ALU.mult)
                    v.drain()

                for j in range(NW):
                    for m in range(2):
                        w = j
                        v.wait_ge(sem["s_pe_dn"], dn_cnt(b, w, m))
                        v.tensor_copy(routed[:, (2 * j + m) * 1024:(2 * j + m + 1) * 1024],
                                      dn_ps[m]).then_inc(sem["s_dve_evac"], 1)

                def evac_expert(eng, i_or_e, sub):
                    for j in range(NW):
                        for m in range(2):
                            if UNROLL_EXPERTS:
                                e = i_or_e
                            else:
                                e = 2 * i_or_e + sub
                            w = (e + 1) * NW + j
                            eng.wait_ge(sem["s_pe_dn"], dn_cnt(b, w, m))
                            rt = routed[:, (2 * j + m) * 1024:(2 * j + m + 1) * 1024]
                            if UNROLL_EXPERTS:
                                cap = comb[:, (2 * j + m) * 160 + e: (2 * j + m) * 160 + e + 1]
                            else:
                                cap = comb[:, bass.ts((2 * j + m) * 160 + e, 1)]
                            eng.scalar_tensor_tensor(out=rt, in0=dn_ps[m], scalar=cap, in1=rt,
                                                     op0=ALU.mult, op1=ALU.add
                                                     ).then_inc(sem["s_dve_evac"], 1)
                if UNROLL_EXPERTS:
                    for e in range(E):
                        evac_expert(v, e, 0)
                else:
                    with v.Fori(0, E // 2) as i:
                        evac_expert(v, i, 0)
                        evac_expert(v, i, 1)

                for ti in range(NT):
                    v.wait_ge(sem["s_act_g"], b * NT + ti + 1)
                    v.tensor_copy(routed[:, ti * 1024:(ti + 1) * 1024], gb_ps[ti % 2]
                                  ).then_inc(sem["s_dve_g"], 1)

                def sm_head(qt):
                    v.wait_ge(sem["s_pe_sc"], b * NT + qt + 1)
                    v.reduce_max(scr["rmaxv"][:], sc_ps, axis=AX)
                    v.drain()
                    v.tensor_scalar_mul(scr["negmax"][:], scr["rmaxv"][:], -1.0 / 32.0)
                    v.drain().then_inc(sem["s_dve_sm1"], 1)

                def sm_mid(qt):
                    v.wait_ge(sem["s_act_exp"], b * NT + qt + 1)
                    v.reciprocal(scr["zinv"][:], scr["rsumv"][:])
                    v.drain()
                    v.tensor_scalar(Pb[:], P[:], scr["zinv"][:], 2.0 / S,
                                    op0=ALU.mult, op1=ALU.subtract)
                    v.drain()
                    v.tensor_add(Pb[:, qt * 128:(qt + 1) * 128],
                                 Pb[:, qt * 128:(qt + 1) * 128], ident[:])
                    v.drain().then_inc(sem["s_dve_pb"], 1)

                def a_tail(qt):
                    for h in range(2):
                        v.wait_ge(sem["s_pe_pt"], 2 * (b * NT + qt) + h + 1)
                        v.tensor_copy(PTs[:, h * 1024:(h + 1) * 1024], ptf[:]
                                      ).then_inc(sem["s_dve_pt"], 1)
                    v.wait_ge(sem["s_pe_av"], b * NT + qt + 1)
                    v.wait_ge(sem["s_xA"], 16 * (b * NT + qt + 1))
                    rt = routed[:, qt * 1024:(qt + 1) * 1024]
                    v.tensor_mul(rt, s_ps, rt)
                    v.drain()
                    v.tensor_add(xin[qt % 2][:], rt, xin[qt % 2][:]).then_inc(sem["s_dve_epi"], 1)

                for qt in range(NT):
                    sm_head(qt)
                    if qt >= 1:
                        a_tail(qt - 1)
                    sm_mid(qt)
                a_tail(NT - 1)

    return nc


def kernel(L0_attn_x, L0_attn_Wq, L0_attn_bq, L0_attn_Wk, L0_attn_bk,
           L0_ffn_router_w, L0_ffn_expert_bias, L0_ffn_w_gate, L0_ffn_w_down,
           L0_ffn_w_gate_shared, L0_ffn_w_down_shared):
    def pmajor(a, nch):
        C = a.shape[0] // 128
        return np.ascontiguousarray(
            a.reshape(C, 128, -1).transpose(1, 0, 2).reshape(128, -1))

    x = np.ascontiguousarray(np.asarray(L0_attn_x, dtype=np.float32))
    wqT = pmajor(np.asarray(L0_attn_Wq, dtype=np.float32).T, KCH)
    wkT = pmajor(np.asarray(L0_attn_Wk, dtype=np.float32).T, KCH)
    bqr = np.ascontiguousarray(np.asarray(L0_attn_bq, dtype=np.float32).reshape(8, 128).T)
    bkr = np.ascontiguousarray(np.asarray(L0_attn_bk, dtype=np.float32).reshape(8, 128).T)
    rwT = pmajor(np.asarray(L0_ffn_router_w, dtype=np.float32).T, KCH)
    biasbc = np.ascontiguousarray(
        np.broadcast_to(np.asarray(L0_ffn_expert_bias, dtype=np.float32), (128, E)))
    wgT3 = np.asarray(L0_ffn_w_gate, dtype=np.float32).transpose(0, 2, 1)
    wgT = np.ascontiguousarray(
        wgT3.reshape(E, KCH, 128, INTER).transpose(0, 2, 1, 3).reshape(E, 128, 4096))
    wdT3 = np.asarray(L0_ffn_w_down, dtype=np.float32).transpose(0, 2, 1)
    wdT = np.ascontiguousarray(
        wdT3.reshape(E, ICH, 128, D).transpose(0, 2, 1, 3).reshape(E, 128, 4096))
    wgsT = pmajor(np.asarray(L0_ffn_w_gate_shared, dtype=np.float32).T, KCH)
    wdsT = pmajor(np.asarray(L0_ffn_w_down_shared, dtype=np.float32).T, ICH)
    identh = np.eye(128, dtype=np.float32).astype(BF)

    nc = build_program()
    in_maps = []
    for c in range(8):
        in_maps.append(dict(x=np.ascontiguousarray(x[2 * c:2 * c + 2]),
                            wqT=wqT, wkT=wkT, bqr=bqr, bkr=bkr, rwT=rwT,
                            biasbc=biasbc, wgT=wgT, wdT=wdT, wgsT=wgsT, wdsT=wdsT,
                            ident=identh))
    res = run_bass_kernel_spmd(nc, in_maps, core_ids=list(range(8)))
    return np.concatenate([np.asarray(r["y"]) for r in res.results], axis=0)


# version 13
# speedup vs baseline: 1.1005x; 1.0109x over previous
import numpy as np
import ml_dtypes
import concourse.bass as bass
import concourse.mybir as mybir
from concourse.bass_utils import run_bass_kernel_spmd

F32 = mybir.dt.float32
BF16 = mybir.dt.bfloat16
AF = mybir.ActivationFunctionType
ALU = mybir.AluOpType
AX = mybir.AxisListType.X
BF = ml_dtypes.bfloat16

NB = 2
S = 2048
D = 1024
E = 160
INTER = 512
NT = 16
NW = 8
KCH = 8
ICH = 4
WPB = 161 * NW
UNROLL_EXPERTS = True
DEBUG = False

SB0 = 16512
OFF_XNAT = SB0 + 0
OFF_ROUT = SB0 + 32768
OFF_XIN = SB0 + 98304
OFF_SCR = SB0 + 106496
OFF_XNT = SB0 + 110592
OFF_SLOTW = SB0 + 143360
OFF_WQT = SB0 + 176128
OFF_SLOTC = SB0 + 192512
OFF_QTT = SB0 + 208896


def build_program():
    nc = bass.Bass("TRN2", target_bir_lowering=False)

    x_d = nc.dram_tensor("x", [NB, S, D], F32, kind="ExternalInput").ap()
    y_d = nc.dram_tensor("y", [NB, S, D], F32, kind="ExternalOutput").ap()
    wqT_d = nc.dram_tensor("wqT", [128, 8192], F32, kind="ExternalInput").ap()
    wkT_d = nc.dram_tensor("wkT", [128, 8192], F32, kind="ExternalInput").ap()
    bq_d = nc.dram_tensor("bqr", [128, 8], F32, kind="ExternalInput").ap()
    bk_d = nc.dram_tensor("bkr", [128, 8], F32, kind="ExternalInput").ap()
    rwT_d = nc.dram_tensor("rwT", [128, 1280], F32, kind="ExternalInput").ap()
    bias_d = nc.dram_tensor("biasbc", [128, E], F32, kind="ExternalInput").ap()
    wg_d = nc.dram_tensor("wgT", [E, 128, 4096], F32, kind="ExternalInput").ap()
    wd_d = nc.dram_tensor("wdT", [E, 128, 4096], F32, kind="ExternalInput").ap()
    wgs_d = nc.dram_tensor("wgsT", [128, 4096], F32, kind="ExternalInput").ap()
    wds_d = nc.dram_tensor("wdsT", [128, 4096], F32, kind="ExternalInput").ap()
    id_d = nc.dram_tensor("ident", [128, 128], BF16, kind="ExternalInput").ap()
    if DEBUG:
        d_xn = nc.dram_tensor("d_xn", [NB, 128, 16384], BF16, kind="ExternalOutput").ap()
        d_xnT = nc.dram_tensor("d_xnT", [NB, 128, 16384], BF16, kind="ExternalOutput").ap()
        d_comb = nc.dram_tensor("d_comb", [NB, 128, 2560], F32, kind="ExternalOutput").ap()
        d_g = nc.dram_tensor("d_g", [NB, 128, 16384], F32, kind="ExternalOutput").ap()
        d_kT = nc.dram_tensor("d_kT", [NB, 128, 16384], BF16, kind="ExternalOutput").ap()
        d_qt0 = nc.dram_tensor("d_qt0", [NB, 128, 1024], BF16, kind="ExternalOutput").ap()
        d_P0 = nc.dram_tensor("d_P0", [NB, 128, 2048], F32, kind="ExternalOutput").ap()
        s_dbg = nc.alloc_semaphore("s_dbg")

    A_ = nc.alloc_sbuf_tensor_at
    xnat = A_("xnat", [128, 16384], BF16, offset=OFF_XNAT)
    routed = A_("routed", [128, 16384], F32, offset=OFF_ROUT)
    xin = [A_(f"xin{i}", [128, 1024], F32, offset=OFF_XIN + i * 4096) for i in range(2)]
    qTt = A_("qTt", [128, 1024], BF16, offset=OFF_SCR)
    bias_sb = A_("bias_sb", [128, 160], F32, offset=OFF_SCR + 2048)
    bq_sb = A_("bq_sb", [128, 8], F32, offset=OFF_SCR + 2688)
    bk_sb = A_("bk_sb", [128, 8], F32, offset=OFF_SCR + 2720)
    ident = A_("ident_sb", [128, 128], BF16, offset=OFF_SCR + 2752)
    sc_base = OFF_SCR + 3008
    names = ["s1", "s2", "meanv", "varv", "msq", "rstdv", "sqv", "rmaxv",
             "negmax", "rsumv", "zinv", "meanrow", "tsumv", "tinvv", "inv32"]
    scr = {}
    for i, n in enumerate(names):
        scr[n] = A_(n, [128, 1], F32, offset=sc_base + 32 * i)
    top8 = A_("top8", [128, 8], F32, offset=sc_base + 32 * len(names))
    xnT = A_("xnT", [128, 16384], BF16, offset=OFF_XNT)
    wg_sb = [A_(f"wg{i}", [128, 4096], BF16, offset=OFF_SLOTW + i * 16384) for i in range(2)]
    wd_sb = [A_(f"wd{i}", [128, 4096], BF16, offset=OFF_SLOTW + 8192 + i * 16384) for i in range(2)]
    kT = A_("kT", [128, 16384], BF16, offset=OFF_SLOTW)
    wqT_sb = A_("wqT_sb", [128, 8192], BF16, offset=OFF_WQT)
    comb = A_("comb", [128, 2560], F32, offset=OFF_SLOTC)
    h1w_offs = [OFF_SLOTC + 10240, OFF_SLOTC + 12288, OFF_SLOTC + 14336, OFF_QTT]
    h1w = [A_(f"h1w{i}", [128, 1024], BF16, offset=h1w_offs[i]) for i in range(4)]
    rwT_sb = A_("rwT_sb", [128, 1280], BF16, offset=OFF_SLOTC + 10240)
    ttmp = A_("ttmp", [128, 1024], F32, offset=OFF_SLOTC)
    probs_pre = A_("probs_pre", [128, 160], F32, offset=SB0 + 210944)
    probs = A_("probs", [128, 160], F32, offset=SB0 + 211584)
    ge = A_("ge", [128, 160], F32, offset=SB0 + 210944)
    wkT_sb = A_("wkT_sb", [128, 8192], BF16, offset=OFF_SLOTC)
    P = A_("P", [128, 2048], F32, offset=OFF_SLOTC)
    Pb = A_("Pb", [128, 2048], BF16, offset=OFF_SLOTC + 8192)
    PTs = A_("PTs", [128, 2048], BF16, offset=OFF_SLOTC + 12288)

    h1_t = nc.alloc_psum_tensor("h1t", [128, 2048], F32)
    dn0_t = nc.alloc_psum_tensor("dn0t", [128, 1024], F32)
    dn1_t = nc.alloc_psum_tensor("dn1t", [128, 1024], F32)
    nc.psum_base = 0
    arena = nc.alloc_psum_tensor("arena", [128, 3072], F32)
    ptb = nc.alloc_psum_tensor("ptb", [128, 2048], BF16)
    nc.psum_base = 0
    _ph0 = nc.alloc_psum_tensor("ph0", [128, 3072], F32)
    ptf = nc.alloc_psum_tensor("ptf", [128, 1024], F32)
    h1_ps = h1_t[:]
    dn_ps = [dn0_t[:], dn1_t[:]]
    lg_ps = arena[:, 0:160]
    qt_ps = arena[:, 0:1024]
    sc_ps = arena[:, 0:2048]
    s_ps = arena[:, 2048:3072]
    kt_ps = [arena[:, 2048:2560], arena[:, 2560:3072]]
    gb_ps = [h1_t[:, 0:1024], h1_t[:, 1024:2048]]

    sem_names = ["s_wdma", "s_xL", "s_xA", "s_out", "s_cst",
                 "s_pe_ln", "s_pe_lg", "s_pe_h1", "s_pe_dn",
                 "s_pe_qt", "s_pe_kt", "s_pe_sc", "s_pe_pt", "s_pe_av",
                 "s_act_rstd", "s_act_sig", "s_act_silu", "s_act_g",
                 "s_act_exp", "s_act_qk", "s_act_qt", "s_act_sq",
                 "s_dve_ln1", "s_dve_xn", "s_dve_xnT", "s_dve_lg",
                 "s_dve_sm1", "s_dve_pb", "s_dve_pt", "s_dve_epi",
                 "s_dve_evac", "s_dve_g"]
    sem = {n: nc.alloc_semaphore(n) for n in sem_names}

    def wdma_rwt_ready(b):
        return 16 * (b * 326 + 1)

    def wdma_shared_ready(b):
        return 16 * (b * 326 + 3)

    def wdma_exp_ready(b, e):
        return 16 * (b * 326 + 2 * e + 6)

    def wdma_qk_ready(b):
        return 16 * (326 * (b + 1))

    def dn_cnt(b, w, m):
        return (b * WPB + w) * 2 + m + 1

    def evac_cnt(b, w, m):
        return (b * WPB + w) * 2 + m + 1

    def silu_cnt(b, w):
        return b * WPB + w + 1


    with nc.Block() as blk:

        @blk.sync
        def _(sy):
            sy.dma_start(out=ident[:], in_=id_d).then_inc(sem["s_cst"], 16)
            sy.dma_start(out=bias_sb[:], in_=bias_d).then_inc(sem["s_cst"], 16)
            sy.dma_start(out=bq_sb[:], in_=bq_d).then_inc(sem["s_cst"], 16)
            sy.dma_start(out=bk_sb[:], in_=bk_d).then_inc(sem["s_cst"], 16)
            for b in range(NB):
                if b > 0:
                    sy.wait_ge(sem["s_out"], 16 * b * NT)
                for t in range(NT):
                    war = b * NT + t - 1
                    if war > 0:
                        sy.wait_ge(sem["s_dve_xn"], war)
                        sy.wait_ge(sem["s_act_sq"], war)
                    sy.dma_start(out=xin[t % 2][:], in_=x_d[b, t * 128:(t + 1) * 128, :]
                                 ).then_inc(sem["s_xL"], 16)
                if DEBUG:
                    sy.wait_ge(sem["s_dve_xnT"], (b + 1) * NT)
                    sy.dma_start(out=d_xn[b], in_=xnat[:]).then_inc(s_dbg, 16)
                    sy.dma_start(out=d_xnT[b], in_=xnT[:]).then_inc(s_dbg, 16)
                    sy.wait_ge(sem["s_dve_evac"], b * 2 * WPB + 17)
                    sy.dma_start(out=d_comb[b], in_=comb[:]).then_inc(s_dbg, 16)
                    sy.wait_ge(sem["s_dve_g"], (b + 1) * NT)
                    sy.dma_start(out=d_g[b], in_=routed[:]).then_inc(s_dbg, 16)
                    sy.wait_ge(sem["s_act_qk"], (b + 1) * 32)
                    sy.dma_start(out=d_kT[b], in_=kT[:]).then_inc(s_dbg, 16)
                    sy.wait_ge(sem["s_act_qt"], b * NT + 1)
                    sy.dma_start(out=d_qt0[b], in_=qTt[:]).then_inc(s_dbg, 16)
                    sy.wait_ge(sem["s_dve_pb"], b * NT + 1)
                    sy.dma_start(out=d_P0[b], in_=P[:]).then_inc(s_dbg, 16)
                sy.wait_ge(sem["s_dve_xn"], (b + 1) * NT)
                for qt in range(NT):
                    war = b * NT + qt - 1
                    if war > 0:
                        sy.wait_ge(sem["s_dve_epi"], war)
                        sy.wait_ge(sem["s_out"], 16 * war)
                    sy.dma_start(out=xin[qt % 2][:], in_=x_d[b, qt * 128:(qt + 1) * 128, :]
                                 ).then_inc(sem["s_xA"], 16)
                    sy.wait_ge(sem["s_dve_epi"], b * NT + qt + 1)
                    sy.dma_start(out=y_d[b, qt * 128:(qt + 1) * 128, :], in_=xin[qt % 2][:]
                                 ).then_inc(sem["s_out"], 16)
            sy.wait_ge(sem["s_out"], 16 * NB * NT)

        @blk.gpsimd
        def _(g):
            for b in range(NB):
                if b > 0:
                    g.wait_ge(sem["s_dve_pt"], 2 * b * NT)
                g.dma_start(out=rwT_sb[:], in_=rwT_d).then_inc(sem["s_wdma"], 16)
                if b > 0:
                    g.wait_ge(sem["s_pe_sc"], b * NT)
                g.dma_start(out=wg_sb[1][:], in_=wgs_d).then_inc(sem["s_wdma"], 16)
                g.dma_start(out=wd_sb[1][:], in_=wds_d).then_inc(sem["s_wdma"], 16)
                if b > 0:
                    g.wait_ge(sem["s_pe_qt"], b * NT)
                g.dma_start(out=wqT_sb[:], in_=wqT_d).then_inc(sem["s_wdma"], 16)
                for e in range(E):
                    if e >= 2:
                        g.wait_ge(sem["s_pe_dn"], 2 * (b * WPB + e * NW))
                    elif e == 1:
                        g.wait_ge(sem["s_pe_dn"], 2 * (b * WPB + NW))
                    elif b > 0:
                        g.wait_ge(sem["s_pe_sc"], b * NT)
                    g.dma_start(out=wg_sb[e % 2][:], in_=wg_d[e]).then_inc(sem["s_wdma"], 16)
                    g.dma_start(out=wd_sb[e % 2][:], in_=wd_d[e]).then_inc(sem["s_wdma"], 16)
                g.wait_ge(sem["s_pe_dn"], 2 * (b + 1) * WPB)
                g.wait_ge(sem["s_dve_evac"], 2 * (b + 1) * WPB)
                g.dma_start(out=wqT_sb[:], in_=wqT_d).then_inc(sem["s_wdma"], 16)
                g.dma_start(out=wkT_sb[:], in_=wkT_d).then_inc(sem["s_wdma"], 16)

        @blk.tensor
        def _(t):
            t.wait_ge(sem["s_cst"], 64)
            for b in range(NB):
                if b > 0:
                    t.wait_ge(sem["s_act_exp"], b * NT)
                    t.wait_ge(sem["s_dve_epi"], b * NT)
                    t.wait_ge(sem["s_dve_pt"], 2 * b * NT)
                    t.wait_ge(sem["s_dve_g"], b * NT)
                for ti in range(NT):
                    t.wait_ge(sem["s_dve_xn"], b * NT + ti + 1)
                    if b * NT + ti > 0:
                        t.wait_ge(sem["s_dve_xnT"], b * NT + ti)
                    for c in range(KCH):
                        mm = t.matmul(ptf[:, c * 128:(c + 1) * 128],
                                      xnat[:, ti * 1024 + c * 128: ti * 1024 + (c + 1) * 128],
                                      ident[:], start=True, stop=True)
                    mm.then_inc(sem["s_pe_ln"], 1)
                t.wait_ge(sem["s_wdma"], wdma_rwt_ready(b))
                for ti in range(NT):
                    t.wait_ge(sem["s_dve_xnT"], b * NT + ti + 1)
                    if b * NT + ti > 0:
                        t.wait_ge(sem["s_dve_lg"], b * NT + ti)
                    for c in range(KCH):
                        mm = t.matmul(lg_ps, xnT[:, c * 2048 + ti * 128: c * 2048 + (ti + 1) * 128],
                                      rwT_sb[:, c * 160:(c + 1) * 160],
                                      start=(c == 0), stop=(c == KCH - 1))
                    mm.then_inc(sem["s_pe_lg"], 1)
                t.wait_ge(sem["s_dve_lg"], (b + 1) * NT)
                t.wait_ge(sem["s_wdma"], wdma_shared_ready(b))

                def gate(eng, wslot, jp):
                    for ich in range(ICH):
                        for c in range(KCH):
                            mm = eng.matmul(
                                h1_ps[:, ich * 512:(ich + 1) * 512],
                                wg_sb[wslot][:, c * 512 + ich * 128: c * 512 + (ich + 1) * 128],
                                xnT[:, c * 2048 + jp * 512: c * 2048 + (jp + 1) * 512],
                                start=(c == 0), stop=(c == KCH - 1))
                    mm.then_inc(sem["s_pe_h1"], 1)

                def down(eng, wslot, par, evac_base):
                    for m in range(2):
                        if evac_base + m > 0:
                            eng.wait_ge(sem["s_dve_evac"], evac_base + m)
                        for ich in range(ICH):
                            for half in range(2):
                                mm = eng.matmul(
                                    dn_ps[m][:, half * 512:(half + 1) * 512],
                                    h1w[par][:, ich * 256 + m * 128: ich * 256 + m * 128 + 128],
                                    wd_sb[wslot][:, ich * 1024 + half * 512: ich * 1024 + (half + 1) * 512],
                                    start=(ich == 0), stop=(ich == ICH - 1))
                        mm.then_inc(sem["s_pe_dn"], 1)

                def down_window(eng, wl):
                    wl_e = wl // NW - 1
                    dslot = 1 if wl_e < 0 else wl_e % 2
                    eng.wait_ge(sem["s_act_silu"], b * WPB + wl + 1)
                    down(eng, dslot, wl % 4, 2 * (b * WPB + wl) - 1)

                def expert_gwins(eng, es):
                    e = es - 1
                    gslot = 1 if e < 0 else e % 2
                    if e >= 0:
                        eng.wait_ge(sem["s_wdma"], wdma_exp_ready(b, e))
                    for jp in range(4):
                        W = es * NW + 2 * jp
                        if W - 2 >= 0:
                            down_window(eng, W - 2)
                        if W > 0:
                            eng.wait_ge(sem["s_act_silu"], b * WPB + W)
                        gate(eng, gslot, jp)
                        if W - 1 >= 0:
                            down_window(eng, W - 1)
                for es in range(161):
                    expert_gwins(t, es)
                down_window(t, WPB - 2)
                down_window(t, WPB - 1)

                t.wait_ge(sem["s_act_silu"], (b + 1) * WPB)
                t.wait_ge(sem["s_dve_evac"], 2 * (b + 1) * WPB)
                t.wait_ge(sem["s_dve_g"], (b + 1) * NT)
                t.wait_ge(sem["s_wdma"], wdma_qk_ready(b))
                for idx in range(32):
                    ec, tc = idx // 4, idx % 4
                    if b * 32 + idx > 1:
                        t.wait_ge(sem["s_act_qk"], b * 32 + idx - 1)
                    for c in range(KCH):
                        mm = t.matmul(kt_ps[idx % 2],
                                      wkT_sb[:, c * 1024 + ec * 128: c * 1024 + (ec + 1) * 128],
                                      xnT[:, c * 2048 + tc * 512: c * 2048 + (tc + 1) * 512],
                                      start=(c == 0), stop=(c == KCH - 1))
                    mm.then_inc(sem["s_pe_kt"], 1)
                def emit_qTt_scores(qt):
                    if DEBUG and qt == 1:
                        t.wait_ge(s_dbg, 112 * b + 96)
                    t.wait_ge(sem["s_act_exp"], b * NT + qt)
                    t.wait_ge(sem["s_act_qt"], b * NT + qt)
                    for ec in range(KCH):
                        for c in range(KCH):
                            mm = t.matmul(qt_ps[:, ec * 128:(ec + 1) * 128],
                                          wqT_sb[:, c * 1024 + ec * 128: c * 1024 + (ec + 1) * 128],
                                          xnT[:, c * 2048 + qt * 128: c * 2048 + (qt + 1) * 128],
                                          start=(c == 0), stop=(c == KCH - 1))
                    mm.then_inc(sem["s_pe_qt"], 1)
                    t.wait_ge(sem["s_act_qt"], b * NT + qt + 1)
                    if qt == 0:
                        t.wait_ge(sem["s_act_qk"], (b + 1) * 32)
                    for kc in range(4):
                        for ec in range(KCH):
                            mm = t.matmul(sc_ps[:, kc * 512:(kc + 1) * 512],
                                          qTt[:, ec * 128:(ec + 1) * 128],
                                          kT[:, ec * 2048 + kc * 512: ec * 2048 + (kc + 1) * 512],
                                          start=(ec == 0), stop=(ec == KCH - 1))
                    mm.then_inc(sem["s_pe_sc"], 1)

                def emit_pt_av(qt):
                    t.wait_ge(sem["s_dve_pb"], b * NT + qt + 1)
                    for h in range(2):
                        t.wait_ge(sem["s_dve_pt"], 2 * (b * NT + qt) + h)
                        for kt_i in range(8 * h, 8 * h + 8):
                            mm = t.matmul(ptf[:, (kt_i % 8) * 128:(kt_i % 8 + 1) * 128],
                                          Pb[:, kt_i * 128:(kt_i + 1) * 128], ident[:],
                                          start=True, stop=True)
                        mm.then_inc(sem["s_pe_pt"], 1)
                    t.wait_ge(sem["s_dve_epi"], b * NT + qt)
                    for dh in range(2):
                        for kt_i in range(NT):
                            if dh == 0 and kt_i in (0, 8):
                                t.wait_ge(sem["s_dve_pt"],
                                          2 * (b * NT + qt) + kt_i // 8 + 1)
                            mm = t.matmul(s_ps[:, dh * 512:(dh + 1) * 512],
                                          PTs[:, kt_i * 128:(kt_i + 1) * 128],
                                          xnat[:, kt_i * 1024 + dh * 512: kt_i * 1024 + dh * 512 + 512],
                                          start=(kt_i == 0), stop=(kt_i == NT - 1))
                    mm.then_inc(sem["s_pe_av"], 1)

                for qt in range(NT):
                    emit_qTt_scores(qt)
                    if qt >= 1:
                        emit_pt_av(qt - 1)
                emit_pt_av(NT - 1)

        @blk.scalar
        def _(s):
            s.wait_ge(sem["s_cst"], 64)
            for b in range(NB):
                for ti in range(NT):
                    if ti == 0 and b > 0:
                        s.wait_ge(sem["s_dve_pb"], b * NT)
                    s.wait_ge(sem["s_xL"], 16 * (b * NT + ti + 1))
                    s.activation(ttmp[:], xin[ti % 2][:], AF.Square,
                                 accum_out=scr["s2"][:]).then_inc(sem["s_act_sq"], 1)
                    s.wait_ge(sem["s_dve_ln1"], b * NT + ti + 1)
                    s.activation(scr["sqv"][:], scr["varv"][:], AF.Sqrt, bias=scr["meanrow"][:]
                                 ).then_inc(sem["s_act_rstd"], 1)
                for ti in range(NT):
                    s.wait_ge(sem["s_dve_lg"], b * NT + ti + 1)
                    s.activation(probs[:], probs_pre[:], AF.Sigmoid
                                 ).then_inc(sem["s_act_sig"], 1)

                h1_3d = h1_t.ap().rearrange("p (i z) -> p i z", i=ICH)
                h1w_3d = [h1w[par].ap().rearrange("p (i z) -> p i z", i=ICH) for par in range(4)]
                GPB = 161 * 4
                for w in range(WPB):
                    hpar = w % 2
                    s.wait_ge(sem["s_pe_h1"], b * GPB + w // 2 + 1)
                    war = 2 * (b * WPB + w) - 6
                    if war > 0:
                        s.wait_ge(sem["s_pe_dn"], war)
                    s.activation(h1w_3d[w % 4], h1_3d[:, :, hpar * 256:(hpar + 1) * 256],
                                 AF.Silu).then_inc(sem["s_act_silu"], 1)

                s.wait_ge(sem["s_dve_evac"], 2 * (b + 1) * WPB)
                for ti in range(NT):
                    if b * NT + ti > 1:
                        s.wait_ge(sem["s_dve_g"], b * NT + ti - 1)
                    s.activation(gb_ps[ti % 2], routed[:, ti * 1024:(ti + 1) * 1024], AF.Silu
                                 ).then_inc(sem["s_act_g"], 1)

                for idx in range(32):
                    ec, tc = idx // 4, idx % 4
                    s.wait_ge(sem["s_pe_kt"], b * 32 + idx + 1)
                    s.add(kT[:, ec * 2048 + tc * 512: ec * 2048 + (tc + 1) * 512],
                          kt_ps[idx % 2], bk_sb[:, ec:ec + 1]).then_inc(sem["s_act_qk"], 1)
                for qt in range(NT):
                    s.wait_ge(sem["s_pe_qt"], b * NT + qt + 1)
                    for ec in range(KCH):
                        ins = s.add(qTt[:, ec * 128:(ec + 1) * 128],
                                    qt_ps[:, ec * 128:(ec + 1) * 128], bq_sb[:, ec:ec + 1])
                    ins.then_inc(sem["s_act_qt"], 1)
                    if DEBUG and qt == 1:
                        s.wait_ge(s_dbg, 112 * b + 112)
                    s.wait_ge(sem["s_pe_sc"], b * NT + qt + 1)
                    s.wait_ge(sem["s_dve_sm1"], b * NT + qt + 1)
                    s.activation(P[:], sc_ps, AF.Exp, bias=scr["negmax"][:], scale=scr["inv32"][:],
                                 accum_out=scr["rsumv"][:]).then_inc(sem["s_act_exp"], 1)

        @blk.vector
        def _(v):
            v.memset(scr["inv32"][:], 1.0 / 32.0)
            v.memset(scr["meanrow"][:], 1e-5)
            v.wait_ge(sem["s_cst"], 64)
            for b in range(NB):
                for ti in range(NT):
                    xb = xin[ti % 2]
                    v.wait_ge(sem["s_xL"], 16 * (b * NT + ti + 1))
                    v.reduce_sum(scr["s1"][:], xb[:], axis=AX)
                    v.drain()
                    v.tensor_scalar_mul(scr["meanv"][:], scr["s1"][:], 1.0 / D)
                    v.drain()
                    v.tensor_mul(scr["msq"][:], scr["meanv"][:], scr["meanv"][:])
                    v.drain()
                    v.wait_ge(sem["s_act_sq"], b * NT + ti + 1)
                    v.scalar_tensor_tensor(out=scr["varv"][:], in0=scr["s2"][:],
                                           scalar=1.0 / D, in1=scr["msq"][:],
                                           op0=ALU.mult, op1=ALU.subtract
                                           ).then_inc(sem["s_dve_ln1"], 1)
                    v.wait_ge(sem["s_act_rstd"], b * NT + ti + 1)
                    v.reciprocal(scr["rstdv"][:], scr["sqv"][:])
                    v.drain()
                    v.tensor_scalar(xnat[:, ti * 1024:(ti + 1) * 1024], xb[:],
                                    scr["meanv"][:], scr["rstdv"][:],
                                    op0=ALU.subtract, op1=ALU.mult).then_inc(sem["s_dve_xn"], 1)
                    v.wait_ge(sem["s_pe_ln"], b * NT + ti + 1)
                    dst = xnT.ap().rearrange("p (c z) -> p c z", c=KCH)[:, :, ti * 128:(ti + 1) * 128]
                    src = ptf.ap().rearrange("p (c z) -> p c z", c=KCH)
                    v.tensor_copy(dst, src).then_inc(sem["s_dve_xnT"], 1)
                for ti in range(NT):
                    v.wait_ge(sem["s_pe_lg"], b * NT + ti + 1)
                    v.tensor_add(probs_pre[:], lg_ps, bias_sb[:]).then_inc(sem["s_dve_lg"], 1)
                    v.wait_ge(sem["s_act_sig"], b * NT + ti + 1)
                    v.max(out=top8[:], in_=probs[:])
                    v.drain()
                    v.reduce_sum(scr["tsumv"][:], top8[:], axis=AX)
                    v.drain()
                    v.tensor_scalar_add(scr["tsumv"][:], scr["tsumv"][:], 1e-9)
                    v.drain()
                    v.reciprocal(scr["tinvv"][:], scr["tsumv"][:])
                    v.drain()
                    v.tensor_scalar_mul(scr["tinvv"][:], scr["tinvv"][:], 2.5)
                    v.drain()
                    v.tensor_scalar(ge[:], probs[:], top8[:, 7:8], None, op0=ALU.is_ge)
                    v.drain()
                    ct = comb[:, ti * 160:(ti + 1) * 160]
                    v.tensor_mul(ct, ge[:], probs[:])
                    v.drain()
                    v.tensor_scalar(ct, ct, scr["tinvv"][:], None, op0=ALU.mult)
                    v.drain()

                for j in range(NW):
                    for m in range(2):
                        w = j
                        v.wait_ge(sem["s_pe_dn"], dn_cnt(b, w, m))
                        v.tensor_copy(routed[:, (2 * j + m) * 1024:(2 * j + m + 1) * 1024],
                                      dn_ps[m]).then_inc(sem["s_dve_evac"], 1)

                def evac_expert(eng, i_or_e, sub):
                    for j in range(NW):
                        for m in range(2):
                            if UNROLL_EXPERTS:
                                e = i_or_e
                            else:
                                e = 2 * i_or_e + sub
                            w = (e + 1) * NW + j
                            eng.wait_ge(sem["s_pe_dn"], dn_cnt(b, w, m))
                            rt = routed[:, (2 * j + m) * 1024:(2 * j + m + 1) * 1024]
                            if UNROLL_EXPERTS:
                                cap = comb[:, (2 * j + m) * 160 + e: (2 * j + m) * 160 + e + 1]
                            else:
                                cap = comb[:, bass.ts((2 * j + m) * 160 + e, 1)]
                            eng.scalar_tensor_tensor(out=rt, in0=dn_ps[m], scalar=cap, in1=rt,
                                                     op0=ALU.mult, op1=ALU.add
                                                     ).then_inc(sem["s_dve_evac"], 1)
                if UNROLL_EXPERTS:
                    for e in range(E):
                        evac_expert(v, e, 0)
                else:
                    with v.Fori(0, E // 2) as i:
                        evac_expert(v, i, 0)
                        evac_expert(v, i, 1)

                for ti in range(NT):
                    v.wait_ge(sem["s_act_g"], b * NT + ti + 1)
                    v.tensor_copy(routed[:, ti * 1024:(ti + 1) * 1024], gb_ps[ti % 2]
                                  ).then_inc(sem["s_dve_g"], 1)

                def sm_head(qt):
                    v.wait_ge(sem["s_pe_sc"], b * NT + qt + 1)
                    v.reduce_max(scr["rmaxv"][:], sc_ps, axis=AX)
                    v.drain()
                    v.tensor_scalar_mul(scr["negmax"][:], scr["rmaxv"][:], -1.0 / 32.0)
                    v.drain().then_inc(sem["s_dve_sm1"], 1)

                def sm_mid(qt):
                    v.wait_ge(sem["s_act_exp"], b * NT + qt + 1)
                    v.reciprocal(scr["zinv"][:], scr["rsumv"][:])
                    v.drain()
                    v.tensor_scalar(Pb[:], P[:], scr["zinv"][:], 2.0 / S,
                                    op0=ALU.mult, op1=ALU.subtract)
                    v.drain()
                    v.tensor_add(Pb[:, qt * 128:(qt + 1) * 128],
                                 Pb[:, qt * 128:(qt + 1) * 128], ident[:])
                    v.drain().then_inc(sem["s_dve_pb"], 1)

                def pt_copies(qt):
                    for h in range(2):
                        v.wait_ge(sem["s_pe_pt"], 2 * (b * NT + qt) + h + 1)
                        v.tensor_copy(PTs[:, h * 1024:(h + 1) * 1024], ptf[:]
                                      ).then_inc(sem["s_dve_pt"], 1)

                def epilogue(qt):
                    v.wait_ge(sem["s_pe_av"], b * NT + qt + 1)
                    v.wait_ge(sem["s_xA"], 16 * (b * NT + qt + 1))
                    rt = routed[:, qt * 1024:(qt + 1) * 1024]
                    v.tensor_mul(rt, s_ps, rt)
                    v.drain()
                    v.tensor_add(xin[qt % 2][:], rt, xin[qt % 2][:]).then_inc(sem["s_dve_epi"], 1)

                for qt in range(NT):
                    sm_head(qt)
                    if qt >= 1:
                        pt_copies(qt - 1)
                    sm_mid(qt)
                    if qt >= 1:
                        epilogue(qt - 1)
                pt_copies(NT - 1)
                epilogue(NT - 1)

    return nc


def kernel(L0_attn_x, L0_attn_Wq, L0_attn_bq, L0_attn_Wk, L0_attn_bk,
           L0_ffn_router_w, L0_ffn_expert_bias, L0_ffn_w_gate, L0_ffn_w_down,
           L0_ffn_w_gate_shared, L0_ffn_w_down_shared):
    def pmajor(a, nch):
        C = a.shape[0] // 128
        return np.ascontiguousarray(
            a.reshape(C, 128, -1).transpose(1, 0, 2).reshape(128, -1))

    x = np.ascontiguousarray(np.asarray(L0_attn_x, dtype=np.float32))
    wqT = pmajor(np.asarray(L0_attn_Wq, dtype=np.float32).T, KCH)
    wkT = pmajor(np.asarray(L0_attn_Wk, dtype=np.float32).T, KCH)
    bqr = np.ascontiguousarray(np.asarray(L0_attn_bq, dtype=np.float32).reshape(8, 128).T)
    bkr = np.ascontiguousarray(np.asarray(L0_attn_bk, dtype=np.float32).reshape(8, 128).T)
    rwT = pmajor(np.asarray(L0_ffn_router_w, dtype=np.float32).T, KCH)
    biasbc = np.ascontiguousarray(
        np.broadcast_to(np.asarray(L0_ffn_expert_bias, dtype=np.float32), (128, E)))
    wgT3 = np.asarray(L0_ffn_w_gate, dtype=np.float32).transpose(0, 2, 1)
    wgT = np.ascontiguousarray(
        wgT3.reshape(E, KCH, 128, INTER).transpose(0, 2, 1, 3).reshape(E, 128, 4096))
    wdT3 = np.asarray(L0_ffn_w_down, dtype=np.float32).transpose(0, 2, 1)
    wdT = np.ascontiguousarray(
        wdT3.reshape(E, ICH, 128, D).transpose(0, 2, 1, 3).reshape(E, 128, 4096))
    wgsT = pmajor(np.asarray(L0_ffn_w_gate_shared, dtype=np.float32).T, KCH)
    wdsT = pmajor(np.asarray(L0_ffn_w_down_shared, dtype=np.float32).T, ICH)
    identh = np.eye(128, dtype=np.float32).astype(BF)

    nc = build_program()
    in_maps = []
    for c in range(8):
        in_maps.append(dict(x=np.ascontiguousarray(x[2 * c:2 * c + 2]),
                            wqT=wqT, wkT=wkT, bqr=bqr, bkr=bkr, rwT=rwT,
                            biasbc=biasbc, wgT=wgT, wdT=wdT, wgsT=wgsT, wdsT=wdsT,
                            ident=identh))
    res = run_bass_kernel_spmd(nc, in_maps, core_ids=list(range(8)))
    return np.concatenate([np.asarray(r["y"]) for r in res.results], axis=0)


# version 15
# speedup vs baseline: 1.1072x; 1.0062x over previous
import numpy as np
import ml_dtypes
import concourse.bass as bass
import concourse.mybir as mybir
from concourse.bass_utils import run_bass_kernel_spmd

F32 = mybir.dt.float32
BF16 = mybir.dt.bfloat16
AF = mybir.ActivationFunctionType
ALU = mybir.AluOpType
AX = mybir.AxisListType.X
BF = ml_dtypes.bfloat16

NB = 2
S = 2048
D = 1024
E = 160
INTER = 512
NT = 16
NW = 8
KCH = 8
ICH = 4
WPB = 161 * NW
UNROLL_EXPERTS = True
DEBUG = False

SB0 = 16512
OFF_XNAT = SB0 + 0
OFF_ROUT = SB0 + 32768
OFF_XIN = SB0 + 98304
OFF_SCR = SB0 + 106496
OFF_XNT = SB0 + 110592
OFF_SLOTW = SB0 + 143360
OFF_WQT = SB0 + 176128
OFF_SLOTC = SB0 + 192512
OFF_QTT = SB0 + 208896


def build_program():
    nc = bass.Bass("TRN2", target_bir_lowering=False)

    x_d = nc.dram_tensor("x", [NB, S, D], F32, kind="ExternalInput").ap()
    y_d = nc.dram_tensor("y", [NB, S, D], F32, kind="ExternalOutput").ap()
    wqT_d = nc.dram_tensor("wqT", [128, 8192], F32, kind="ExternalInput").ap()
    wkT_d = nc.dram_tensor("wkT", [128, 8192], F32, kind="ExternalInput").ap()
    bq_d = nc.dram_tensor("bqr", [128, 8], F32, kind="ExternalInput").ap()
    bk_d = nc.dram_tensor("bkr", [128, 8], F32, kind="ExternalInput").ap()
    rwT_d = nc.dram_tensor("rwT", [128, 1280], F32, kind="ExternalInput").ap()
    bias_d = nc.dram_tensor("biasbc", [128, E], F32, kind="ExternalInput").ap()
    wg_d = nc.dram_tensor("wgT", [E, 128, 4096], F32, kind="ExternalInput").ap()
    wd_d = nc.dram_tensor("wdT", [E, 128, 4096], F32, kind="ExternalInput").ap()
    wgs_d = nc.dram_tensor("wgsT", [128, 4096], F32, kind="ExternalInput").ap()
    wds_d = nc.dram_tensor("wdsT", [128, 4096], F32, kind="ExternalInput").ap()
    id_d = nc.dram_tensor("ident", [128, 128], BF16, kind="ExternalInput").ap()
    if DEBUG:
        d_xn = nc.dram_tensor("d_xn", [NB, 128, 16384], BF16, kind="ExternalOutput").ap()
        d_xnT = nc.dram_tensor("d_xnT", [NB, 128, 16384], BF16, kind="ExternalOutput").ap()
        d_comb = nc.dram_tensor("d_comb", [NB, 128, 2560], F32, kind="ExternalOutput").ap()
        d_g = nc.dram_tensor("d_g", [NB, 128, 16384], F32, kind="ExternalOutput").ap()
        d_kT = nc.dram_tensor("d_kT", [NB, 128, 16384], BF16, kind="ExternalOutput").ap()
        d_qt0 = nc.dram_tensor("d_qt0", [NB, 128, 1024], BF16, kind="ExternalOutput").ap()
        d_P0 = nc.dram_tensor("d_P0", [NB, 128, 2048], F32, kind="ExternalOutput").ap()
        s_dbg = nc.alloc_semaphore("s_dbg")

    A_ = nc.alloc_sbuf_tensor_at
    xnat = A_("xnat", [128, 16384], BF16, offset=OFF_XNAT)
    routed = A_("routed", [128, 16384], F32, offset=OFF_ROUT)
    xin = [A_(f"xin{i}", [128, 1024], F32, offset=OFF_XIN + i * 4096) for i in range(2)]
    qTt = A_("qTt", [128, 1024], BF16, offset=OFF_SCR)
    bias_sb = A_("bias_sb", [128, 160], F32, offset=OFF_SCR + 2048)
    bq_sb = A_("bq_sb", [128, 8], F32, offset=OFF_SCR + 2688)
    bk_sb = A_("bk_sb", [128, 8], F32, offset=OFF_SCR + 2720)
    ident = A_("ident_sb", [128, 128], BF16, offset=OFF_SCR + 2752)
    sc_base = OFF_SCR + 3008
    names = ["s1", "s2", "meanv", "varv", "msq", "rstdv", "sqv", "rmaxv",
             "negmax", "rsumv", "zinv", "meanrow", "tsumv", "tinvv", "inv32",
             "s2b", "meanv1", "varv1", "sqv1"]
    scr = {}
    for i, n in enumerate(names):
        scr[n] = A_(n, [128, 1], F32, offset=sc_base + 32 * i)
    top8 = A_("top8", [128, 8], F32, offset=sc_base + 32 * len(names))
    xnT = A_("xnT", [128, 16384], BF16, offset=OFF_XNT)
    wg_sb = [A_(f"wg{i}", [128, 4096], BF16, offset=OFF_SLOTW + i * 16384) for i in range(2)]
    wd_sb = [A_(f"wd{i}", [128, 4096], BF16, offset=OFF_SLOTW + 8192 + i * 16384) for i in range(2)]
    kT = A_("kT", [128, 16384], BF16, offset=OFF_SLOTW)
    wqT_sb = A_("wqT_sb", [128, 8192], BF16, offset=OFF_WQT)
    comb = A_("comb", [128, 2560], F32, offset=OFF_SLOTC)
    h1w_offs = [OFF_SLOTC + 10240, OFF_SLOTC + 12288, OFF_SLOTC + 14336, OFF_QTT]
    h1w = [A_(f"h1w{i}", [128, 1024], BF16, offset=h1w_offs[i]) for i in range(4)]
    rwT_sb = A_("rwT_sb", [128, 1280], BF16, offset=OFF_SLOTC + 10240)
    ttmp = A_("ttmp", [128, 1024], F32, offset=OFF_SLOTC)
    probs_pre = A_("probs_pre", [128, 160], F32, offset=SB0 + 210944)
    probs = A_("probs", [128, 160], F32, offset=SB0 + 211584)
    ge = A_("ge", [128, 160], F32, offset=SB0 + 210944)
    wkT_sb = A_("wkT_sb", [128, 8192], BF16, offset=OFF_SLOTC)
    P = A_("P", [128, 2048], F32, offset=OFF_SLOTC)
    Pb = A_("Pb", [128, 2048], BF16, offset=OFF_SLOTC + 8192)
    PTs = A_("PTs", [128, 2048], BF16, offset=OFF_SLOTC + 12288)

    h1_t = nc.alloc_psum_tensor("h1t", [128, 2048], F32)
    dn0_t = nc.alloc_psum_tensor("dn0t", [128, 1024], F32)
    dn1_t = nc.alloc_psum_tensor("dn1t", [128, 1024], F32)
    nc.psum_base = 0
    arena = nc.alloc_psum_tensor("arena", [128, 3072], F32)
    ptb = nc.alloc_psum_tensor("ptb", [128, 2048], BF16)
    nc.psum_base = 0
    _ph0 = nc.alloc_psum_tensor("ph0", [128, 3072], F32)
    ptf = nc.alloc_psum_tensor("ptf", [128, 1024], F32)
    h1_ps = h1_t[:]
    dn_ps = [dn0_t[:], dn1_t[:]]
    lg_ps = arena[:, 0:160]
    qt_ps = arena[:, 0:1024]
    sc_ps = arena[:, 0:2048]
    s_ps = arena[:, 2048:3072]
    kt_ps = [arena[:, 2048:2560], arena[:, 2560:3072]]
    gb_ps = [h1_t[:, 0:1024], h1_t[:, 1024:2048]]

    sem_names = ["s_wdma", "s_xL", "s_xA", "s_out", "s_cst",
                 "s_pe_ln", "s_pe_lg", "s_pe_h1", "s_pe_dn",
                 "s_pe_qt", "s_pe_kt", "s_pe_sc", "s_pe_pt", "s_pe_av",
                 "s_act_rstd", "s_act_sig", "s_act_silu", "s_act_g",
                 "s_act_exp", "s_act_qk", "s_act_qt", "s_act_sq",
                 "s_dve_ln1", "s_dve_xn", "s_dve_xnT", "s_dve_lg",
                 "s_dve_sm1", "s_dve_pb", "s_dve_pt", "s_dve_epi",
                 "s_dve_evac", "s_dve_g"]
    sem = {n: nc.alloc_semaphore(n) for n in sem_names}

    def wdma_rwt_ready(b):
        return 16 * (b * 326 + 1)

    def wdma_shared_ready(b):
        return 16 * (b * 326 + 3)

    def wdma_exp_ready(b, e):
        return 16 * (b * 326 + 2 * e + 6)

    def wdma_qk_ready(b):
        return 16 * (326 * (b + 1))

    def dn_cnt(b, w, m):
        return (b * WPB + w) * 2 + m + 1

    def evac_cnt(b, w, m):
        return (b * WPB + w) * 2 + m + 1

    def silu_cnt(b, w):
        return b * WPB + w + 1


    with nc.Block() as blk:

        @blk.sync
        def _(sy):
            sy.dma_start(out=ident[:], in_=id_d).then_inc(sem["s_cst"], 16)
            sy.dma_start(out=bias_sb[:], in_=bias_d).then_inc(sem["s_cst"], 16)
            sy.dma_start(out=bq_sb[:], in_=bq_d).then_inc(sem["s_cst"], 16)
            sy.dma_start(out=bk_sb[:], in_=bk_d).then_inc(sem["s_cst"], 16)
            for b in range(NB):
                if b > 0:
                    sy.wait_ge(sem["s_out"], 16 * b * NT)
                for t in range(NT):
                    war = b * NT + t - 1
                    if war > 0:
                        sy.wait_ge(sem["s_dve_xn"], war)
                        sy.wait_ge(sem["s_act_sq"], war)
                    sy.dma_start(out=xin[t % 2][:], in_=x_d[b, t * 128:(t + 1) * 128, :]
                                 ).then_inc(sem["s_xL"], 16)
                if DEBUG:
                    sy.wait_ge(sem["s_dve_xnT"], (b + 1) * NT)
                    sy.dma_start(out=d_xn[b], in_=xnat[:]).then_inc(s_dbg, 16)
                    sy.dma_start(out=d_xnT[b], in_=xnT[:]).then_inc(s_dbg, 16)
                    sy.wait_ge(sem["s_dve_evac"], b * 2 * WPB + 17)
                    sy.dma_start(out=d_comb[b], in_=comb[:]).then_inc(s_dbg, 16)
                    sy.wait_ge(sem["s_dve_g"], (b + 1) * NT)
                    sy.dma_start(out=d_g[b], in_=routed[:]).then_inc(s_dbg, 16)
                    sy.wait_ge(sem["s_act_qk"], (b + 1) * 32)
                    sy.dma_start(out=d_kT[b], in_=kT[:]).then_inc(s_dbg, 16)
                    sy.wait_ge(sem["s_act_qt"], b * NT + 1)
                    sy.dma_start(out=d_qt0[b], in_=qTt[:]).then_inc(s_dbg, 16)
                    sy.wait_ge(sem["s_dve_pb"], b * NT + 1)
                    sy.dma_start(out=d_P0[b], in_=P[:]).then_inc(s_dbg, 16)
                sy.wait_ge(sem["s_dve_xn"], (b + 1) * NT)
                for qt in range(NT):
                    war = b * NT + qt - 1
                    if war > 0:
                        sy.wait_ge(sem["s_dve_epi"], war)
                        sy.wait_ge(sem["s_out"], 16 * war)
                    sy.dma_start(out=xin[qt % 2][:], in_=x_d[b, qt * 128:(qt + 1) * 128, :]
                                 ).then_inc(sem["s_xA"], 16)
                    sy.wait_ge(sem["s_dve_epi"], b * NT + qt + 1)
                    sy.dma_start(out=y_d[b, qt * 128:(qt + 1) * 128, :], in_=xin[qt % 2][:]
                                 ).then_inc(sem["s_out"], 16)
            sy.wait_ge(sem["s_out"], 16 * NB * NT)

        @blk.gpsimd
        def _(g):
            for b in range(NB):
                if b > 0:
                    g.wait_ge(sem["s_dve_pt"], 2 * b * NT)
                g.dma_start(out=rwT_sb[:], in_=rwT_d).then_inc(sem["s_wdma"], 16)
                if b > 0:
                    g.wait_ge(sem["s_pe_sc"], b * NT)
                g.dma_start(out=wg_sb[1][:], in_=wgs_d).then_inc(sem["s_wdma"], 16)
                g.dma_start(out=wd_sb[1][:], in_=wds_d).then_inc(sem["s_wdma"], 16)
                if b > 0:
                    g.wait_ge(sem["s_pe_qt"], b * NT)
                g.dma_start(out=wqT_sb[:], in_=wqT_d).then_inc(sem["s_wdma"], 16)
                for e in range(E):
                    if e >= 2:
                        g.wait_ge(sem["s_pe_dn"], 2 * (b * WPB + e * NW))
                    elif e == 1:
                        g.wait_ge(sem["s_pe_dn"], 2 * (b * WPB + NW))
                    elif b > 0:
                        g.wait_ge(sem["s_pe_sc"], b * NT)
                    g.dma_start(out=wg_sb[e % 2][:], in_=wg_d[e]).then_inc(sem["s_wdma"], 16)
                    g.dma_start(out=wd_sb[e % 2][:], in_=wd_d[e]).then_inc(sem["s_wdma"], 16)
                g.wait_ge(sem["s_pe_dn"], 2 * (b + 1) * WPB)
                g.wait_ge(sem["s_dve_evac"], 2 * (b + 1) * WPB)
                g.dma_start(out=wqT_sb[:], in_=wqT_d).then_inc(sem["s_wdma"], 16)
                g.dma_start(out=wkT_sb[:], in_=wkT_d).then_inc(sem["s_wdma"], 16)

        @blk.tensor
        def _(t):
            t.wait_ge(sem["s_cst"], 64)
            for b in range(NB):
                if b > 0:
                    t.wait_ge(sem["s_act_exp"], b * NT)
                    t.wait_ge(sem["s_dve_epi"], b * NT)
                    t.wait_ge(sem["s_dve_pt"], 2 * b * NT)
                    t.wait_ge(sem["s_dve_g"], b * NT)
                for ti in range(NT):
                    t.wait_ge(sem["s_dve_xn"], b * NT + ti + 1)
                    if b * NT + ti > 0:
                        t.wait_ge(sem["s_dve_xnT"], b * NT + ti)
                    for c in range(KCH):
                        mm = t.matmul(ptf[:, c * 128:(c + 1) * 128],
                                      xnat[:, ti * 1024 + c * 128: ti * 1024 + (c + 1) * 128],
                                      ident[:], start=True, stop=True)
                    mm.then_inc(sem["s_pe_ln"], 1)
                t.wait_ge(sem["s_wdma"], wdma_rwt_ready(b))
                for ti in range(NT):
                    t.wait_ge(sem["s_dve_xnT"], b * NT + ti + 1)
                    if b * NT + ti > 0:
                        t.wait_ge(sem["s_dve_lg"], b * NT + ti)
                    for c in range(KCH):
                        mm = t.matmul(lg_ps, xnT[:, c * 2048 + ti * 128: c * 2048 + (ti + 1) * 128],
                                      rwT_sb[:, c * 160:(c + 1) * 160],
                                      start=(c == 0), stop=(c == KCH - 1))
                    mm.then_inc(sem["s_pe_lg"], 1)
                t.wait_ge(sem["s_dve_lg"], (b + 1) * NT)
                t.wait_ge(sem["s_wdma"], wdma_shared_ready(b))

                def gate(eng, wslot, jp):
                    for ich in range(ICH):
                        for c in range(KCH):
                            mm = eng.matmul(
                                h1_ps[:, ich * 512:(ich + 1) * 512],
                                wg_sb[wslot][:, c * 512 + ich * 128: c * 512 + (ich + 1) * 128],
                                xnT[:, c * 2048 + jp * 512: c * 2048 + (jp + 1) * 512],
                                start=(c == 0), stop=(c == KCH - 1))
                    mm.then_inc(sem["s_pe_h1"], 1)

                def down(eng, wslot, par, evac_base):
                    for m in range(2):
                        if evac_base + m > 0:
                            eng.wait_ge(sem["s_dve_evac"], evac_base + m)
                        for ich in range(ICH):
                            for half in range(2):
                                mm = eng.matmul(
                                    dn_ps[m][:, half * 512:(half + 1) * 512],
                                    h1w[par][:, ich * 256 + m * 128: ich * 256 + m * 128 + 128],
                                    wd_sb[wslot][:, ich * 1024 + half * 512: ich * 1024 + (half + 1) * 512],
                                    start=(ich == 0), stop=(ich == ICH - 1))
                        mm.then_inc(sem["s_pe_dn"], 1)

                def down_window(eng, wl):
                    wl_e = wl // NW - 1
                    dslot = 1 if wl_e < 0 else wl_e % 2
                    eng.wait_ge(sem["s_act_silu"], b * WPB + wl + 1)
                    down(eng, dslot, wl % 4, 2 * (b * WPB + wl) - 1)

                def expert_gwins(eng, es):
                    e = es - 1
                    gslot = 1 if e < 0 else e % 2
                    if e >= 0:
                        eng.wait_ge(sem["s_wdma"], wdma_exp_ready(b, e))
                    for jp in range(4):
                        W = es * NW + 2 * jp
                        if W - 2 >= 0:
                            down_window(eng, W - 2)
                        if W > 0:
                            eng.wait_ge(sem["s_act_silu"], b * WPB + W)
                        gate(eng, gslot, jp)
                        if W - 1 >= 0:
                            down_window(eng, W - 1)
                for es in range(161):
                    expert_gwins(t, es)
                down_window(t, WPB - 2)
                down_window(t, WPB - 1)

                t.wait_ge(sem["s_act_silu"], (b + 1) * WPB)
                t.wait_ge(sem["s_dve_evac"], 2 * (b + 1) * WPB)
                t.wait_ge(sem["s_dve_g"], (b + 1) * NT)
                t.wait_ge(sem["s_wdma"], wdma_qk_ready(b))
                for idx in range(32):
                    ec, tc = idx // 4, idx % 4
                    if b * 32 + idx > 1:
                        t.wait_ge(sem["s_act_qk"], b * 32 + idx - 1)
                    for c in range(KCH):
                        mm = t.matmul(kt_ps[idx % 2],
                                      wkT_sb[:, c * 1024 + ec * 128: c * 1024 + (ec + 1) * 128],
                                      xnT[:, c * 2048 + tc * 512: c * 2048 + (tc + 1) * 512],
                                      start=(c == 0), stop=(c == KCH - 1))
                    mm.then_inc(sem["s_pe_kt"], 1)
                def emit_qTt_scores(qt):
                    if DEBUG and qt == 1:
                        t.wait_ge(s_dbg, 112 * b + 96)
                    t.wait_ge(sem["s_act_exp"], b * NT + qt)
                    t.wait_ge(sem["s_act_qt"], b * NT + qt)
                    for ec in range(KCH):
                        for c in range(KCH):
                            mm = t.matmul(qt_ps[:, ec * 128:(ec + 1) * 128],
                                          wqT_sb[:, c * 1024 + ec * 128: c * 1024 + (ec + 1) * 128],
                                          xnT[:, c * 2048 + qt * 128: c * 2048 + (qt + 1) * 128],
                                          start=(c == 0), stop=(c == KCH - 1))
                    mm.then_inc(sem["s_pe_qt"], 1)
                    t.wait_ge(sem["s_act_qt"], b * NT + qt + 1)
                    if qt == 0:
                        t.wait_ge(sem["s_act_qk"], (b + 1) * 32)
                    for kc in range(4):
                        for ec in range(KCH):
                            mm = t.matmul(sc_ps[:, kc * 512:(kc + 1) * 512],
                                          qTt[:, ec * 128:(ec + 1) * 128],
                                          kT[:, ec * 2048 + kc * 512: ec * 2048 + (kc + 1) * 512],
                                          start=(ec == 0), stop=(ec == KCH - 1))
                    mm.then_inc(sem["s_pe_sc"], 1)

                def emit_pt_av(qt):
                    t.wait_ge(sem["s_dve_pb"], b * NT + qt + 1)
                    for h in range(2):
                        t.wait_ge(sem["s_dve_pt"], 2 * (b * NT + qt) + h)
                        for kt_i in range(8 * h, 8 * h + 8):
                            mm = t.matmul(ptf[:, (kt_i % 8) * 128:(kt_i % 8 + 1) * 128],
                                          Pb[:, kt_i * 128:(kt_i + 1) * 128], ident[:],
                                          start=True, stop=True)
                        mm.then_inc(sem["s_pe_pt"], 1)
                    t.wait_ge(sem["s_dve_epi"], b * NT + qt)
                    for dh in range(2):
                        for kt_i in range(NT):
                            if dh == 0 and kt_i in (0, 8):
                                t.wait_ge(sem["s_dve_pt"],
                                          2 * (b * NT + qt) + kt_i // 8 + 1)
                            mm = t.matmul(s_ps[:, dh * 512:(dh + 1) * 512],
                                          PTs[:, kt_i * 128:(kt_i + 1) * 128],
                                          xnat[:, kt_i * 1024 + dh * 512: kt_i * 1024 + dh * 512 + 512],
                                          start=(kt_i == 0), stop=(kt_i == NT - 1))
                    mm.then_inc(sem["s_pe_av"], 1)

                for qt in range(NT):
                    emit_qTt_scores(qt)
                    if qt >= 1:
                        emit_pt_av(qt - 1)
                emit_pt_av(NT - 1)

        @blk.scalar
        def _(s):
            s.wait_ge(sem["s_cst"], 64)
            s2_p2 = [scr["s2"], scr["s2b"]]
            vv_p2 = [scr["varv"], scr["varv1"]]
            sq_p2 = [scr["sqv"], scr["sqv1"]]
            for b in range(NB):
                for ti in range(NT):
                    p, k = ti % 2, b * NT + ti
                    if ti == 0 and b > 0:
                        s.wait_ge(sem["s_dve_pb"], b * NT)
                    s.wait_ge(sem["s_xL"], 16 * (k + 1))
                    if k >= 2:
                        s.wait_ge(sem["s_dve_ln1"], k - 1)
                    s.activation(ttmp[:], xin[p][:], AF.Square,
                                 accum_out=s2_p2[p][:]).then_inc(sem["s_act_sq"], 1)
                    s.wait_ge(sem["s_dve_ln1"], k + 1)
                    if k >= 2:
                        s.wait_ge(sem["s_dve_xn"], k - 1)
                    s.activation(sq_p2[p][:], vv_p2[p][:], AF.Sqrt, bias=scr["meanrow"][:]
                                 ).then_inc(sem["s_act_rstd"], 1)
                for ti in range(NT):
                    s.wait_ge(sem["s_dve_lg"], b * NT + ti + 1)
                    s.activation(probs[:], probs_pre[:], AF.Sigmoid
                                 ).then_inc(sem["s_act_sig"], 1)

                h1_3d = h1_t.ap().rearrange("p (i z) -> p i z", i=ICH)
                h1w_3d = [h1w[par].ap().rearrange("p (i z) -> p i z", i=ICH) for par in range(4)]
                GPB = 161 * 4
                for w in range(WPB):
                    hpar = w % 2
                    s.wait_ge(sem["s_pe_h1"], b * GPB + w // 2 + 1)
                    war = 2 * (b * WPB + w) - 6
                    if war > 0:
                        s.wait_ge(sem["s_pe_dn"], war)
                    s.activation(h1w_3d[w % 4], h1_3d[:, :, hpar * 256:(hpar + 1) * 256],
                                 AF.Silu).then_inc(sem["s_act_silu"], 1)

                s.wait_ge(sem["s_dve_evac"], 2 * (b + 1) * WPB)
                for ti in range(NT):
                    if b * NT + ti > 1:
                        s.wait_ge(sem["s_dve_g"], b * NT + ti - 1)
                    s.activation(gb_ps[ti % 2], routed[:, ti * 1024:(ti + 1) * 1024], AF.Silu
                                 ).then_inc(sem["s_act_g"], 1)

                for idx in range(32):
                    ec, tc = idx // 4, idx % 4
                    s.wait_ge(sem["s_pe_kt"], b * 32 + idx + 1)
                    s.add(kT[:, ec * 2048 + tc * 512: ec * 2048 + (tc + 1) * 512],
                          kt_ps[idx % 2], bk_sb[:, ec:ec + 1]).then_inc(sem["s_act_qk"], 1)
                for qt in range(NT):
                    s.wait_ge(sem["s_pe_qt"], b * NT + qt + 1)
                    for ec in range(KCH):
                        ins = s.add(qTt[:, ec * 128:(ec + 1) * 128],
                                    qt_ps[:, ec * 128:(ec + 1) * 128], bq_sb[:, ec:ec + 1])
                    ins.then_inc(sem["s_act_qt"], 1)
                    if DEBUG and qt == 1:
                        s.wait_ge(s_dbg, 112 * b + 112)
                    s.wait_ge(sem["s_pe_sc"], b * NT + qt + 1)
                    s.wait_ge(sem["s_dve_sm1"], b * NT + qt + 1)
                    s.activation(P[:], sc_ps, AF.Exp, bias=scr["negmax"][:], scale=scr["inv32"][:],
                                 accum_out=scr["rsumv"][:]).then_inc(sem["s_act_exp"], 1)

        @blk.vector
        def _(v):
            v.memset(scr["inv32"][:], 1.0 / 32.0)
            v.memset(scr["meanrow"][:], 1e-5)
            v.wait_ge(sem["s_cst"], 64)
            for b in range(NB):
                mv_p = [scr["meanv"], scr["meanv1"]]
                vv_p = [scr["varv"], scr["varv1"]]
                s2_p = [scr["s2"], scr["s2b"]]
                sq_p = [scr["sqv"], scr["sqv1"]]

                def l_head(ti):
                    p, k = ti % 2, b * NT + ti
                    v.wait_ge(sem["s_xL"], 16 * (k + 1))
                    v.reduce_sum(scr["s1"][:], xin[p][:], axis=AX)
                    v.drain()
                    v.tensor_scalar_mul(mv_p[p][:], scr["s1"][:], 1.0 / D)
                    v.drain()
                    v.tensor_mul(scr["msq"][:], mv_p[p][:], mv_p[p][:])
                    v.drain()
                    v.wait_ge(sem["s_act_sq"], k + 1)
                    if k >= 2:
                        v.wait_ge(sem["s_act_rstd"], k - 1)
                    v.scalar_tensor_tensor(out=vv_p[p][:], in0=s2_p[p][:],
                                           scalar=1.0 / D, in1=scr["msq"][:],
                                           op0=ALU.mult, op1=ALU.subtract
                                           ).then_inc(sem["s_dve_ln1"], 1)

                def l_tail(ti):
                    p, k = ti % 2, b * NT + ti
                    v.wait_ge(sem["s_act_rstd"], k + 1)
                    v.reciprocal(scr["rstdv"][:], sq_p[p][:])
                    v.drain()
                    v.tensor_scalar(xnat[:, ti * 1024:(ti + 1) * 1024], xin[p][:],
                                    mv_p[p][:], scr["rstdv"][:],
                                    op0=ALU.subtract, op1=ALU.mult).then_inc(sem["s_dve_xn"], 1)
                    v.wait_ge(sem["s_pe_ln"], k + 1)
                    dst = xnT.ap().rearrange("p (c z) -> p c z", c=KCH)[:, :, ti * 128:(ti + 1) * 128]
                    src = ptf.ap().rearrange("p (c z) -> p c z", c=KCH)
                    v.tensor_copy(dst, src).then_inc(sem["s_dve_xnT"], 1)

                for ti in range(NT):
                    l_head(ti)
                    if ti >= 1:
                        l_tail(ti - 1)
                l_tail(NT - 1)
                for ti in range(NT):
                    v.wait_ge(sem["s_pe_lg"], b * NT + ti + 1)
                    v.tensor_add(probs_pre[:], lg_ps, bias_sb[:]).then_inc(sem["s_dve_lg"], 1)
                    v.wait_ge(sem["s_act_sig"], b * NT + ti + 1)
                    v.max(out=top8[:], in_=probs[:])
                    v.drain()
                    v.reduce_sum(scr["tsumv"][:], top8[:], axis=AX)
                    v.drain()
                    v.tensor_scalar_add(scr["tsumv"][:], scr["tsumv"][:], 1e-9)
                    v.drain()
                    v.reciprocal(scr["tinvv"][:], scr["tsumv"][:])
                    v.drain()
                    v.tensor_scalar_mul(scr["tinvv"][:], scr["tinvv"][:], 2.5)
                    v.drain()
                    v.tensor_scalar(ge[:], probs[:], top8[:, 7:8], None, op0=ALU.is_ge)
                    v.drain()
                    ct = comb[:, ti * 160:(ti + 1) * 160]
                    v.tensor_mul(ct, ge[:], probs[:])
                    v.drain()
                    v.tensor_scalar(ct, ct, scr["tinvv"][:], None, op0=ALU.mult)
                    v.drain()

                for j in range(NW):
                    for m in range(2):
                        w = j
                        v.wait_ge(sem["s_pe_dn"], dn_cnt(b, w, m))
                        v.tensor_copy(routed[:, (2 * j + m) * 1024:(2 * j + m + 1) * 1024],
                                      dn_ps[m]).then_inc(sem["s_dve_evac"], 1)

                def evac_expert(eng, i_or_e, sub):
                    for j in range(NW):
                        for m in range(2):
                            if UNROLL_EXPERTS:
                                e = i_or_e
                            else:
                                e = 2 * i_or_e + sub
                            w = (e + 1) * NW + j
                            eng.wait_ge(sem["s_pe_dn"], dn_cnt(b, w, m))
                            rt = routed[:, (2 * j + m) * 1024:(2 * j + m + 1) * 1024]
                            if UNROLL_EXPERTS:
                                cap = comb[:, (2 * j + m) * 160 + e: (2 * j + m) * 160 + e + 1]
                            else:
                                cap = comb[:, bass.ts((2 * j + m) * 160 + e, 1)]
                            eng.scalar_tensor_tensor(out=rt, in0=dn_ps[m], scalar=cap, in1=rt,
                                                     op0=ALU.mult, op1=ALU.add
                                                     ).then_inc(sem["s_dve_evac"], 1)
                if UNROLL_EXPERTS:
                    for e in range(E):
                        evac_expert(v, e, 0)
                else:
                    with v.Fori(0, E // 2) as i:
                        evac_expert(v, i, 0)
                        evac_expert(v, i, 1)

                for ti in range(NT):
                    v.wait_ge(sem["s_act_g"], b * NT + ti + 1)
                    v.tensor_copy(routed[:, ti * 1024:(ti + 1) * 1024], gb_ps[ti % 2]
                                  ).then_inc(sem["s_dve_g"], 1)

                def sm_head(qt):
                    v.wait_ge(sem["s_pe_sc"], b * NT + qt + 1)
                    v.reduce_max(scr["rmaxv"][:], sc_ps, axis=AX)
                    v.drain()
                    v.tensor_scalar_mul(scr["negmax"][:], scr["rmaxv"][:], -1.0 / 32.0)
                    v.drain().then_inc(sem["s_dve_sm1"], 1)

                def sm_mid(qt):
                    v.wait_ge(sem["s_act_exp"], b * NT + qt + 1)
                    v.reciprocal(scr["zinv"][:], scr["rsumv"][:])
                    v.drain()
                    v.tensor_scalar(Pb[:], P[:], scr["zinv"][:], 2.0 / S,
                                    op0=ALU.mult, op1=ALU.subtract)
                    v.drain()
                    v.tensor_add(Pb[:, qt * 128:(qt + 1) * 128],
                                 Pb[:, qt * 128:(qt + 1) * 128], ident[:])
                    v.drain().then_inc(sem["s_dve_pb"], 1)

                def a_tail(qt):
                    for h in range(2):
                        v.wait_ge(sem["s_pe_pt"], 2 * (b * NT + qt) + h + 1)
                        v.tensor_copy(PTs[:, h * 1024:(h + 1) * 1024], ptf[:]
                                      ).then_inc(sem["s_dve_pt"], 1)
                    v.wait_ge(sem["s_pe_av"], b * NT + qt + 1)
                    v.wait_ge(sem["s_xA"], 16 * (b * NT + qt + 1))
                    rt = routed[:, qt * 1024:(qt + 1) * 1024]
                    v.tensor_mul(rt, s_ps, rt)
                    v.drain()
                    v.tensor_add(xin[qt % 2][:], rt, xin[qt % 2][:]).then_inc(sem["s_dve_epi"], 1)

                for qt in range(NT):
                    sm_head(qt)
                    if qt >= 1:
                        a_tail(qt - 1)
                    sm_mid(qt)
                a_tail(NT - 1)

    return nc


def kernel(L0_attn_x, L0_attn_Wq, L0_attn_bq, L0_attn_Wk, L0_attn_bk,
           L0_ffn_router_w, L0_ffn_expert_bias, L0_ffn_w_gate, L0_ffn_w_down,
           L0_ffn_w_gate_shared, L0_ffn_w_down_shared):
    def pmajor(a, nch):
        C = a.shape[0] // 128
        return np.ascontiguousarray(
            a.reshape(C, 128, -1).transpose(1, 0, 2).reshape(128, -1))

    x = np.ascontiguousarray(np.asarray(L0_attn_x, dtype=np.float32))
    wqT = pmajor(np.asarray(L0_attn_Wq, dtype=np.float32).T, KCH)
    wkT = pmajor(np.asarray(L0_attn_Wk, dtype=np.float32).T, KCH)
    bqr = np.ascontiguousarray(np.asarray(L0_attn_bq, dtype=np.float32).reshape(8, 128).T)
    bkr = np.ascontiguousarray(np.asarray(L0_attn_bk, dtype=np.float32).reshape(8, 128).T)
    rwT = pmajor(np.asarray(L0_ffn_router_w, dtype=np.float32).T, KCH)
    biasbc = np.ascontiguousarray(
        np.broadcast_to(np.asarray(L0_ffn_expert_bias, dtype=np.float32), (128, E)))
    wgT3 = np.asarray(L0_ffn_w_gate, dtype=np.float32).transpose(0, 2, 1)
    wgT = np.ascontiguousarray(
        wgT3.reshape(E, KCH, 128, INTER).transpose(0, 2, 1, 3).reshape(E, 128, 4096))
    wdT3 = np.asarray(L0_ffn_w_down, dtype=np.float32).transpose(0, 2, 1)
    wdT = np.ascontiguousarray(
        wdT3.reshape(E, ICH, 128, D).transpose(0, 2, 1, 3).reshape(E, 128, 4096))
    wgsT = pmajor(np.asarray(L0_ffn_w_gate_shared, dtype=np.float32).T, KCH)
    wdsT = pmajor(np.asarray(L0_ffn_w_down_shared, dtype=np.float32).T, ICH)
    identh = np.eye(128, dtype=np.float32).astype(BF)

    nc = build_program()
    in_maps = []
    for c in range(8):
        in_maps.append(dict(x=np.ascontiguousarray(x[2 * c:2 * c + 2]),
                            wqT=wqT, wkT=wkT, bqr=bqr, bkr=bkr, rwT=rwT,
                            biasbc=biasbc, wgT=wgT, wdT=wdT, wgsT=wgsT, wdsT=wdsT,
                            ident=identh))
    res = run_bass_kernel_spmd(nc, in_maps, core_ids=list(range(8)))
    return np.concatenate([np.asarray(r["y"]) for r in res.results], axis=0)
